# version 33
# baseline (speedup 1.0000x reference)
import numpy as np
import ml_dtypes
from contextlib import ExitStack

import concourse.bass as bass
import concourse.mybir as mybir
from concourse.bass_utils import run_bass_kernel_spmd

F32 = mybir.dt.float32
BF16 = mybir.dt.bfloat16
AF = mybir.ActivationFunctionType
ALU = mybir.AluOpType
AX = mybir.AxisListType

S = 4096
DM = 1024
H = 16
NT = 32
NCH = 8
MC = 8
KT = 32
E = 8
FF = 512
FS = 1024
QSCALE = 0.125
LSCALE = 0.125
CSHIFT = 40.0
EPS = 1e-5
SLOPES = [0.5 ** h for h in range(H)]
KEEP = [[kt for kt in range(KT)
         if SLOPES[h] * (4095.0 - (kt * 128 + 127)) <= 30.0] for h in range(H)]

DEBUG = False
PHASES = "PABCD"
A_NCH = NCH
WNAMES = ("q", "k", "v", "o")


def host_consts():
    eye_bf = np.eye(128, dtype=ml_dtypes.bfloat16)
    eye_f = np.eye(128, dtype=np.float32)
    ab = np.zeros((128, H * KT), np.float32)
    kk = np.arange(128, dtype=np.float64)
    for h in range(H):
        for kt in range(KT):
            ab[:, h * KT + kt] = (SLOPES[h] * ((kt * 128 + kk) - 4095.0)
                                  - CSHIFT).astype(np.float32)
    ones_bf = np.ones((1, 128), ml_dtypes.bfloat16)
    oh = np.zeros((8, E * 128), ml_dtypes.bfloat16)
    for e in range(E):
        oh[e, e * 128:(e + 1) * 128] = 1
    sel2 = np.zeros((2, 128), ml_dtypes.bfloat16)
    sel2[0, 0:64] = 1
    sel2[1, 64:128] = 1
    sel16 = np.zeros((16, 8 * 128), ml_dtypes.bfloat16)
    for mc in range(8):
        sel16[2 * mc, mc * 128:mc * 128 + 64] = 1
        sel16[2 * mc + 1, mc * 128 + 64:mc * 128 + 128] = 1
    return dict(c_idbf=np.ascontiguousarray(eye_bf),
                c_idf=np.ascontiguousarray(eye_f),
                c_ab=np.ascontiguousarray(ab),
                c_ones=np.ascontiguousarray(ones_bf),
                c_oh=np.ascontiguousarray(oh),
                c_sel2=np.ascontiguousarray(sel2),
                c_sel16=np.ascontiguousarray(sel16))


class K:
    pass


def _mk_sems(k, names):
    for nm in names:
        setattr(k, nm, k.Sem(nm))


def build_nc():
    nc = bass.Bass("TRN2", target_bir_lowering=False)
    k = K()
    k.nc = nc
    st = ExitStack()
    k.st = st

    k.x = nc.dram_tensor("x", [S, DM], F32, kind="ExternalInput")
    k.wq = {}
    for n in WNAMES:
        k.wq[n] = dict(
            W=nc.dram_tensor(f"W{n}", [DM, DM], F32, kind="ExternalInput"),
            b=nc.dram_tensor(f"b{n}", [DM], F32, kind="ExternalInput"),
            A=nc.dram_tensor(f"A{n}", [8, DM], F32, kind="ExternalInput"),
            B=nc.dram_tensor(f"B{n}", [DM, 8], F32, kind="ExternalInput"))
    k.rz_d = nc.dram_tensor("rz", [1], F32, kind="ExternalInput")
    k.wr_d = nc.dram_tensor("Wr", [DM, E], F32, kind="ExternalInput")
    k.wg_d = nc.dram_tensor("Wg", [E, DM, FF], F32, kind="ExternalInput")
    k.wu_d = nc.dram_tensor("Wu", [E, DM, FF], F32, kind="ExternalInput")
    k.wd_d = nc.dram_tensor("Wd", [E, FF, DM], F32, kind="ExternalInput")
    k.sg_d = nc.dram_tensor("Sg", [DM, FS], F32, kind="ExternalInput")
    k.su_d = nc.dram_tensor("Su", [DM, FS], F32, kind="ExternalInput")
    k.sd_d = nc.dram_tensor("Sd", [FS, DM], F32, kind="ExternalInput")
    k.eb_d = nc.dram_tensor("eb", [E], F32, kind="ExternalInput")
    k.c_idbf = nc.dram_tensor("c_idbf", [128, 128], BF16, kind="ExternalInput")
    k.c_idf = nc.dram_tensor("c_idf", [128, 128], F32, kind="ExternalInput")
    k.c_ab = nc.dram_tensor("c_ab", [128, H * KT], F32, kind="ExternalInput")
    k.c_ones = nc.dram_tensor("c_ones", [1, 128], BF16, kind="ExternalInput")
    k.c_oh = nc.dram_tensor("c_oh", [8, E * 128], BF16, kind="ExternalInput")
    k.c_sel2 = nc.dram_tensor("c_sel2", [2, 128], BF16, kind="ExternalInput")
    k.c_sel16 = nc.dram_tensor("c_sel16", [16, 8 * 128], BF16, kind="ExternalInput")
    k.sums_hbm = nc.dram_tensor("sums_hbm", [H, S], F32, kind="Internal")
    k.wbf_g = nc.dram_tensor("wbf_g", [E, 128, 4096], BF16, kind="Internal")
    k.wbf_u = nc.dram_tensor("wbf_u", [E, 128, 4096], BF16, kind="Internal")
    k.wbf_d = nc.dram_tensor("wbf_d", [8, 128, 4096], BF16, kind="Internal")
    k.wbf_sd = nc.dram_tensor("wbf_sd", [8, 128, 1024], BF16, kind="Internal")
    k.wbf_sg = nc.dram_tensor("wbf_sg", [8, 128, 1024], BF16, kind="Internal")
    k.wbf_su = nc.dram_tensor("wbf_su", [8, 128, 1024], BF16, kind="Internal")

    sk = "ExternalOutput" if DEBUG else "Internal"
    k.qs = nc.dram_tensor("qs", [128, MC * S], BF16, kind=sk)
    k.ks = nc.dram_tensor("ks", [128, MC * S], BF16, kind=sk)
    k.vs = nc.dram_tensor("vs", [128, KT * H * 65], BF16, kind=sk)
    k.n1s = nc.dram_tensor("n1s", [128, MC * S], BF16, kind=sk)
    k.o1s = nc.dram_tensor("o1s", [S, DM], F32, kind=sk)
    if DEBUG:
        k.rts = nc.dram_tensor("rts", [128, MC * S], BF16, kind="ExternalOutput")
    k.y = nc.dram_tensor("y", [S, DM], F32, kind="ExternalOutput")

    k.idbf = st.enter_context(nc.sbuf_tensor("idbf", [128, 128], BF16))
    k.idf = st.enter_context(nc.sbuf_tensor("idf", [128, 128], F32))
    k.onesb = st.enter_context(nc.sbuf_tensor("onesb", [1, 128], BF16))
    k.ohsb = st.enter_context(nc.sbuf_tensor("ohsb", [8, E * 128], BF16))
    k.sel2 = st.enter_context(nc.sbuf_tensor("sel2", [2, 128], BF16))
    k.biast = {n: st.enter_context(nc.sbuf_tensor(f"bias_{n}", [128, 8], F32))
               for n in WNAMES}
    k.rzb = st.enter_context(nc.sbuf_tensor("rzb", [128, 1], F32))
    k.ebsb = st.enter_context(nc.sbuf_tensor("ebsb", [8, 1], F32))

    class Sem:
        def __init__(self, name):
            self.h = st.enter_context(nc.semaphore(name))
            self.n = 0

        def inc(self, inst, kk=1):
            inst.then_inc(self.h, kk)
            self.n += kk
            return self.n

    k.Sem = Sem
    k.s_out = Sem("s_out")

    stBC = ExitStack()
    k.m0sb = stBC.enter_context(nc.sbuf_tensor("m0sb", [128, NT], F32))
    k.r0sb = stBC.enter_context(nc.sbuf_tensor("r0sb", [128, NT], F32))
    k.nr0sb = stBC.enter_context(nc.sbuf_tensor("nr0sb", [128, NT], F32))
    k.wot = stBC.enter_context(nc.sbuf_tensor("wot", [128, MC * DM], BF16))
    k.absb = stBC.enter_context(nc.sbuf_tensor("absb", [128, H * KT], F32))
    k.stBC = stBC

    emit_PA(k)
    k.rtn = stBC.enter_context(nc.sbuf_tensor("rtn", [128, MC * S], BF16))
    if "A" not in PHASES:
        pass
    if "B" in PHASES:
        emit_B(k)
    if "C" in PHASES:
        emit_C(k)
    stBC.close()
    if "D" in PHASES:
        emit_D(k)
    else:
        with nc.Block("stub") as blk:
            @blk.sync
            def _(sp):
                sp.dma_start(out=k.y[0:128, 0:128], in_=k.idf[:, :]
                             ).then_inc(k.s_out.h, 16)
                k.s_out.n += 16
                sp.wait_ge(k.s_out.h, k.s_out.n)
    st.close()
    return nc


def emit_PA(k):
    nc = k.nc
    Sem = k.Sem
    stA = ExitStack()
    sb = lambda name, shape, dt: stA.enter_context(nc.sbuf_tensor(name, shape, dt))
    ps = lambda name, shape, dt: stA.enter_context(nc.psum_tensor(name, shape, dt))

    wqt = sb("wqt", [128, MC * DM], BF16)
    wkt = sb("wkt", [128, MC * DM], BF16)
    wvt = sb("wvt", [128, MC * DM], BF16)
    WT = {"q": wqt, "k": wkt, "v": wvt, "o": k.wot}
    stP = ExitStack()
    sbP = lambda name, shape, dt: stP.enter_context(nc.sbuf_tensor(name, shape, dt))
    scrA = sb("scrA", [128, DM], BF16)
    n0a = [sb(f"n0a{i}", [128, 4 * DM], BF16) for i in range(2)]
    n0t = [sb(f"n0t{i}", [128, MC * 512], BF16) for i in range(2)]
    qtc = [sb(f"qtc{i}", [128, MC * 512], BF16) for i in range(2)]
    ktc = [sb(f"ktc{i}", [128, MC * 512], BF16) for i in range(2)]
    vcb = [sb(f"vcb{i}", [128, 4 * H * 65], BF16) for i in range(2)]
    vtmp = [sb(f"vtmp{i}", [128, 512], BF16) for i in range(2)]
    ssums = [sb(f"ssums{i}", [128, 4], F32) for i in range(2)]
    ssqs = [sb(f"ssqs{i}", [128, 4], F32) for i in range(2)]
    mst = [sb(f"mst{i}", [128, 4], F32) for i in range(2)]
    vpes = [sb(f"vpes{i}", [128, 4], F32) for i in range(2)]
    recs = [sb(f"recs{i}", [128, 4], F32) for i in range(2)]
    r0s = [sb(f"r0s{i}", [128, 4], F32) for i in range(2)]
    nws = [sb(f"nws{i}", [128, 4], F32) for i in range(2)]

    t1s = sbP("t1s", [128, MC * DM], BF16)
    wld = [sbP(f"wld{i}", [128, MC * DM], BF16) for i in range(2)]
    asb = sbP("asb", [8, DM], BF16)
    bsb = sbP("bsb", [128, 64], BF16)
    btr = sbP("btr", [8, DM], BF16)

    pstn = [ps(f"pstn{i}", [128, 1024], BF16) for i in range(2)]
    pstv = [ps(f"pstv{i}", [128, 1024], BF16) for i in range(2)]
    psq = [ps(f"psq{i}", [128, 512], F32) for i in range(2)]

    _mk_sems(k, ["s_cl", "s_wl", "s_ppb", "s_bte", "s_ppm", "s_t1e", "s_ppv",
                 "s_we", "s_rc", "s_sqa", "s_n0", "s_tpn",
                 "s_ten", "s_tpv", "s_tev", "s_mmq", "s_qe", "s_vt", "s_ms"])
    k.s_xc2 = [Sem(f"s_xc{i}") for i in range(2)]
    k.s_spd2 = [Sem("s_spd0"), Sem("s_spd1")]

    with nc.Block("phP") as blk:
        @blk.sync
        def _(sp):
            for src, dst in ((k.c_idbf, k.idbf), (k.c_idf, k.idf),
                             (k.c_ab, k.absb), (k.c_ones, k.onesb),
                             (k.c_oh, k.ohsb), (k.c_sel2, k.sel2)):
                sp.dma_start(out=dst[:, :], in_=src[:, :]).then_inc(k.s_cl.h, 16)
                k.s_cl.n += 16
            sp.dma_start(out=k.ebsb[:, :],
                         in_=k.eb_d.rearrange("(p one) -> p one", one=1),
                         allow_slow_non_contiguous=True).then_inc(k.s_cl.h, 16)
            k.s_cl.n += 16
            sp.dma_start(out=k.rzb[:, :], in_=bass.AP(k.rz_d, 0, [[0, 128], [1, 1]])
                         ).then_inc(k.s_cl.h, 16)
            k.s_cl.n += 16
            for n in WNAMES:
                sp.dma_start(out=k.biast[n][:, :],
                             in_=k.wq[n]["b"].rearrange("(c p) -> p c", p=128),
                             allow_slow_non_contiguous=True).then_inc(k.s_cl.h, 16)
                k.s_cl.n += 16

        @blk.gpsimd
        def _(g):
            for wi, n in enumerate(WNAMES):
                if wi > 0:
                    g.wait_ge(k.s_ppm.h, 16 * wi)
                    g.wait_ge(k.s_ppb.h, 8 * wi)
                g.dma_start(out=asb[0:8, :], in_=k.wq[n]["A"][:, :]
                            ).then_inc(k.s_wl.h, 16)
                k.s_wl.n += 16
                g.dma_start(out=bsb[:, :].rearrange("p (c r) -> p c r", c=8),
                            in_=k.wq[n]["B"].rearrange("(c p) r -> p c r", p=128),
                            allow_slow_non_contiguous=True).then_inc(k.s_wl.h, 16)
                k.s_wl.n += 16
                if wi >= 2:
                    g.wait_ge(k.s_ppv.h, 64 * (wi - 1))
                g.dma_start(out=wld[wi % 2][:, :].rearrange("p (c d) -> p c d", c=8),
                            in_=k.wq[n]["W"].rearrange("(c p) d -> p c d", p=128)
                            ).then_inc(k.s_wl.h, 16)
                k.s_wl.n += 16

        @blk.tensor
        def _(te):
            te.wait_ge(k.s_cl.h, k.s_cl.n)
            for wi, n in enumerate(WNAMES):
                te.wait_ge(k.s_wl.h, 48 * (wi + 1))
                for c in range(8):
                    gi = 8 * wi + c
                    if gi > 0:
                        te.wait_ge(k.s_bte.h, gi)
                    k.s_ppb.inc(te.transpose(pstn[0][0:8, 0:128],
                                             bsb[:, c * 8:(c + 1) * 8],
                                             k.idbf[:, :]))
                te.wait_ge(k.s_bte.h, 8 * (wi + 1))
                for i in range(16):
                    mcd, half = i // 2, i % 2
                    gi = 16 * wi + i
                    if gi >= 2:
                        te.wait_ge(k.s_t1e.h, gi - 1)
                    k.s_ppm.inc(te.matmul(psq[i % 2][:, :],
                                          asb[0:8, mcd * 128:(mcd + 1) * 128],
                                          btr[0:8, half * 512:(half + 1) * 512],
                                          start=True, stop=True))
                te.wait_ge(k.s_wl.h, 48 * (wi + 1))
                for g in range(8):
                    gl = 8 * wi + g
                    if gl >= 2:
                        te.wait_ge(k.s_we.h, gl - 1)
                    for ncc in range(8):
                        k.s_ppv.inc(te.transpose(
                            pstv[gl % 2][:, ncc * 128:(ncc + 1) * 128],
                            wld[wi % 2][:, ncc * 1024 + g * 128:
                                        ncc * 1024 + g * 128 + 128],
                            k.idbf[:, :]))

        @blk.scalar
        def _(sc):
            for wi, n in enumerate(WNAMES):
                for c in range(8):
                    gi = 8 * wi + c
                    sc.wait_ge(k.s_ppb.h, gi + 1)
                    k.s_bte.inc(sc.copy(out=btr[0:8, c * 128:(c + 1) * 128],
                                        in_=pstn[0][0:8, 0:128]))
                t1scale = LSCALE * (QSCALE if n == "q" else 1.0)
                for i in range(16):
                    mcd, half = i // 2, i % 2
                    gi = 16 * wi + i
                    sc.wait_ge(k.s_ppm.h, gi + 1)
                    if wi >= 1:
                        sc.wait_ge(k.s_we.h, 8 * (wi - 1) + mcd + 1)
                    k.s_t1e.inc(sc.activation(
                        out=t1s[:, mcd * 1024 + half * 512:
                                mcd * 1024 + (half + 1) * 512],
                        in_=psq[i % 2][:, :], func=AF.Copy, scale=t1scale))

        @blk.vector
        def _(v):
            v.wait_ge(k.s_cl.h, k.s_cl.n)
            v.tensor_scalar(k.biast["q"][:, :], k.biast["q"][:, :], QSCALE, None,
                            ALU.mult, ALU.bypass)
            for wi, n in enumerate(WNAMES):
                wscale = QSCALE if n == "q" else 1.0
                for g in range(8):
                    gl = 8 * wi + g
                    v.wait_ge(k.s_ppv.h, 64 * wi + 8 * (g + 1))
                    v.wait_ge(k.s_t1e.h, 16 * wi + 2 * g + 2)
                    k.s_we.inc(v.scalar_tensor_tensor(
                        out=WT[n][:, g * 1024:(g + 1) * 1024],
                        in0=pstv[gl % 2][:, :], scalar=wscale,
                        in1=t1s[:, g * 1024:(g + 1) * 1024],
                        op0=ALU.mult, op1=ALU.add))

    if DEBUG:
        s_wdbg = Sem("s_wdbg")
        wts_d = nc.dram_tensor("wts", [4, 128, MC * DM], BF16, kind="ExternalOutput")
        with nc.Block("phPdbg") as blk:
            @blk.sync
            def _(sp):
                for wi, n in enumerate(WNAMES):
                    sp.dma_start(out=wts_d[wi, :, :], in_=WT[n][:, :]
                                 ).then_inc(s_wdbg.h, 16)
                sp.wait_ge(s_wdbg.h, 64)

    if "A" not in PHASES:
        stA.close()
        return

    stP.close()
    xa2 = [sb(f"xa2_{i}", [128, 4 * DM], F32) for i in range(2)]

    def grp_evt(c, i):
        if i // 8 < 2:
            return (k.s_qe, 16 * c + i + 1)
        return (k.s_vt, 8 * c + (i % 8) + 1)

    with nc.Block("phA") as blk:
        @blk.sync
        def _(sp):
            def spill(cp):
                sp.wait_ge(k.s_qe.h, 16 * (cp + 1))
                sp.wait_ge(k.s_tev.h, 8 * (cp + 1))
                sem = k.s_spd2[cp % 2]
                for srcb, dst in ((qtc, k.qs), (ktc, k.ks)):
                    sem.inc(sp.dma_start(
                        out=dst[:, :].rearrange("p (m t2) -> p m t2", m=MC)[
                            :, :, cp * 512:(cp + 1) * 512],
                        in_=srcb[cp % 2][:, :].rearrange("p (m t2) -> p m t2", m=MC)
                    ), 16)
                sem.inc(sp.dma_start(
                    out=k.vs[:, :].rearrange("p (kt j) -> p kt j", kt=KT)[
                        :, 4 * cp:4 * cp + 4, :],
                    in_=vcb[cp % 2][:, :].rearrange("p (kt j) -> p kt j", kt=4)
                ), 16)

            for c in range(1, A_NCH):
                spill(c - 1)
            spill(A_NCH - 1)

        @blk.gpsimd
        def _(g):
            for c in range(min(2, A_NCH)):
                k.s_xc2[c % 2].inc(g.dma_start(
                    out=xa2[c % 2][:, :].rearrange("p (tt d) -> p tt d", tt=4),
                    in_=k.x[c * 512:(c + 1) * 512, :].rearrange(
                        "(tt p) d -> p tt d", p=128)), 16)
            for c in range(A_NCH):
                if c >= 2:
                    g.wait_ge(k.s_spd2[c % 2].h, 48 * ((c - 2) // 2 + 1))
                k.s_ms.inc(g.memset(vcb[c % 2][:, :], 1.0))
                if c + 2 < A_NCH:
                    g.wait_ge(k.s_n0.h, 4 * (c + 1))
                    g.wait_ge(k.s_sqa.h, 4 * (c + 1))
                    k.s_xc2[c % 2].inc(g.dma_start(
                        out=xa2[c % 2][:, :].rearrange("p (tt d) -> p tt d", tt=4),
                        in_=k.x[(c + 2) * 512:(c + 3) * 512, :].rearrange(
                            "(tt p) d -> p tt d", p=128)), 16)

        @blk.scalar
        def _(sc):
            def squares(c):
                sc.wait_ge(k.s_xc2[c % 2].h, 16 * (c // 2 + 1))
                if c >= 2:
                    sc.wait_ge(k.s_rc.h, 2 * c - 3)
                for tt in range(4):
                    sc.drain()
                    k.s_sqa.inc(sc.activation(
                        out=scrA[:, :], in_=xa2[c % 2][:, tt * DM:(tt + 1) * DM],
                        func=AF.Square, accum_out=ssqs[c % 2][:, tt:tt + 1]))

            def sqrtc(c):
                sc.wait_ge(k.s_rc.h, 2 * c + 1)
                sc.activation(out=recs[c % 2][:, :], in_=vpes[c % 2][:, :],
                              func=AF.Ln)
                sc.drain()
                k.s_rc.inc(sc.activation(out=nws[c % 2][:, :],
                                         in_=recs[c % 2][:, :],
                                         func=AF.Exp, scale=-0.5))

            for c in range(min(2, A_NCH)):
                squares(c)
                sqrtc(c)
            for c in range(A_NCH):
                if c >= 2:
                    sc.wait_ge(k.s_mmq.h, 24 * (c - 1))
                for tt in range(4):
                    t = 4 * c + tt
                    sc.wait_ge(k.s_tpn.h, 8 * (t + 1))
                    k.s_ten.inc(sc.copy(
                        out=n0t[c % 2][:, :].rearrange(
                            "p (kc t2) -> p kc t2", kc=MC)[
                            :, :, tt * 128:(tt + 1) * 128],
                        in_=pstn[t % 2][:, :].rearrange(
                            "p (kc t2) -> p kc t2", kc=MC)))
                if c >= 2:
                    sc.wait_ge(k.s_spd2[c % 2].h, 48 * ((c - 2) // 2 + 1))
                sc.wait_ge(k.s_ms.h, c + 1)
                for i in range(24):
                    w, mc = i // 8, i % 8
                    n = WNAMES[w]
                    sc.wait_ge(k.s_mmq.h, 24 * c + i + 1)
                    if w < 2:
                        dst = (qtc if w == 0 else ktc)[c % 2]
                        k.s_qe.inc(sc.activation(
                            out=dst[:, mc * 512:(mc + 1) * 512],
                            in_=psq[i % 2][:, :], func=AF.Identity,
                            bias=k.biast[n][:, mc:mc + 1]))
                    else:
                        k.s_vt.inc(sc.activation(
                            out=vtmp[mc % 2][:, :],
                            in_=psq[i % 2][:, :], func=AF.Identity,
                            bias=k.biast[n][:, mc:mc + 1]))
                        gv = 8 * c + mc
                        sc.wait_ge(k.s_tpv.h, 4 * (gv + 1))
                        k.s_tev.inc(sc.copy(
                            out=vcb[c % 2][:, :].rearrange(
                                "p (tt hh j) -> p tt hh j", tt=4, hh=H)[
                                :, :, 2 * mc:2 * mc + 2, 0:64],
                            in_=pstv[gv % 2][:, 0:512].rearrange(
                                "p (tt hh j) -> p tt hh j", tt=4, hh=2)))
                    if c + 2 < A_NCH:
                        if i == 7:
                            squares(c + 2)
                        elif i == 19:
                            sqrtc(c + 2)
            del squares, sqrtc

        @blk.vector
        def _(v):
            for c in range(A_NCH):
                if c >= 2:
                    v.wait_ge(k.s_tpn.h, 32 * (c - 1))
                v.wait_ge(k.s_xc2[c % 2].h, 16 * (c // 2 + 1))
                p = c % 2
                for tt in range(4):
                    v.reduce_sum(ssums[p][:, tt:tt + 1],
                                 xa2[c % 2][:, tt * DM:(tt + 1) * DM], axis=AX.X)
                v.wait_ge(k.s_sqa.h, 4 * (c + 1))
                v.drain()
                v.tensor_scalar(mst[p][:, :], ssums[p][:, :], 1.0 / DM, None,
                                ALU.mult, ALU.bypass)
                v.tensor_scalar(vpes[p][:, :], ssqs[p][:, :], 1.0 / DM, EPS,
                                ALU.mult, ALU.add)
                v.drain()
                v.tensor_tensor(nws[p][:, :], mst[p][:, :], mst[p][:, :], ALU.mult)
                v.drain()
                k.s_rc.inc(v.tensor_tensor(vpes[p][:, :], vpes[p][:, :],
                                           nws[p][:, :], ALU.subtract))
                v.wait_ge(k.s_rc.h, 2 * c + 2)
                v.tensor_copy(k.m0sb[:, 4 * c:4 * c + 4], mst[p][:, :])
                v.tensor_copy(k.r0sb[:, 4 * c:4 * c + 4], nws[p][:, :])
                v.scalar_tensor_tensor(out=k.nr0sb[:, 4 * c:4 * c + 4],
                                       in0=mst[p][:, :], scalar=-1.0,
                                       in1=nws[p][:, :],
                                       op0=ALU.mult, op1=ALU.mult)
                v.drain()
                for tt in range(4):
                    t = 4 * c + tt
                    k.s_n0.inc(v.tensor_scalar(
                        n0a[c % 2][:, tt * DM:(tt + 1) * DM],
                        xa2[c % 2][:, tt * DM:(tt + 1) * DM],
                        mst[p][:, tt:tt + 1], nws[p][:, tt:tt + 1],
                        ALU.subtract, ALU.mult))

        @blk.tensor
        def _(te):
            for c in range(A_NCH):
                for tt in range(4):
                    t = 4 * c + tt
                    te.wait_ge(k.s_n0.h, t + 1)
                    if t >= 2:
                        te.wait_ge(k.s_ten.h, t - 1)
                    for kc in range(MC):
                        k.s_tpn.inc(te.transpose(
                            pstn[t % 2][:, kc * 128:(kc + 1) * 128],
                            n0a[c % 2][:, tt * DM + kc * 128:tt * DM + kc * 128 + 128],
                            k.idbf[:, :]))
                te.wait_ge(k.s_ten.h, 4 * (c + 1))
                for i in range(24):
                    w, mc = i // 8, i % 8
                    gi = 24 * c + i
                    if gi >= 2:
                        pc, pii = (gi - 2) // 24, (gi - 2) % 24
                        sem, val = grp_evt(pc, pii)
                        te.wait_ge(sem.h, val)
                    wtt = WT[WNAMES[w]]
                    for kc in range(MC):
                        mm = te.matmul(psq[gi % 2][:, :],
                                       wtt[:, kc * 1024 + mc * 128:
                                           kc * 1024 + (mc + 1) * 128],
                                       n0t[c % 2][:, kc * 512:(kc + 1) * 512],
                                       start=(kc == 0), stop=(kc == MC - 1))
                    k.s_mmq.inc(mm)
                    if w == 2:
                        te.wait_ge(k.s_vt.h, 8 * c + mc + 1)
                        gv = 8 * c + mc
                        if gv >= 2:
                            te.wait_ge(k.s_tev.h, gv - 1)
                        for tt in range(4):
                            vi = 32 * c + mc * 4 + tt
                            k.s_tpv.inc(te.transpose(
                                pstv[gv % 2][:, (vi % 4) * 128:(vi % 4) * 128 + 128],
                                vtmp[mc % 2][:, tt * 128:(tt + 1) * 128],
                                k.idbf[:, :]))

    stA.close()


def emit_B(k):
    nc = k.nc
    Sem = k.Sem
    stB = ExitStack()
    sb = lambda name, shape, dt: stB.enter_context(nc.sbuf_tensor(name, shape, dt))
    ps = lambda name, shape, dt: stB.enter_context(nc.psum_tensor(name, shape, dt))

    ktb = sb("ktb", [128, 4 * S], BF16)
    vhb = sb("vhb", [128, KT * 520], BF16)
    qbb = [sb(f"qbb{i}", [128, 4 * 1024], BF16) for i in range(2)]
    ptb = [[sb(f"ptb{x}{i}", [128, 512], BF16) for i in range(3)]
           for x in "AB"]
    sstage = [sb(f"sstage{i}", [1, 1024], F32) for i in range(2)]
    pss = [[ps(f"pss{x}{i}", [128, 512], F32) for i in range(2)] for x in "AB"]
    pso2 = [ps(f"pso{x}", [128, 1024], F32) for x in "AB"]

    _mk_sems(k, ["s_kv", "s_qrl", "s_sml"])
    k.s_qb2 = [Sem("s_qb0"), Sem("s_qb1")]
    k.s_sd2 = [Sem("s_sd0"), Sem("s_sd1")]
    k.s_smX = [Sem("s_smA"), Sem("s_smB")]
    k.s_sexX = [Sem("s_sexA"), Sem("s_sexB")]
    k.s_savX = [Sem("s_savA"), Sem("s_savB")]

    with nc.Block("phB") as blk:
        @blk.sync
        def _(sp):
            sp.wait_ge(k.s_spd2[0].h, k.s_spd2[0].n)
            sp.wait_ge(k.s_spd2[1].h, k.s_spd2[1].n)
            def sums_dma(p, qb, jh):
                hi = (p * 4 + qb) * 8 + jh
                h = 8 * p + jh
                sp.wait_ge(k.s_sml.h, 2 * hi + 2)
                k.s_sd2[hi % 2].inc(sp.dma_start(
                    out=k.sums_hbm[h, qb * 1024:(qb + 1) * 1024],
                    in_=sstage[hi % 2][0:1, :]), 16)
            for p in range(2):
                if p == 1:
                    sp.wait_ge(k.s_sml.h, 2 * 32)
                sp.dma_start(
                    out=ktb[:, :].rearrange("p (m t2) -> p m t2", m=4),
                    in_=k.ks[:, :].rearrange("p (m t2) -> p m t2", m=MC)[
                        :, 4 * p:4 * p + 4, :]).then_inc(k.s_kv.h, 16)
                k.s_kv.n += 16
                sp.dma_start(
                    out=vhb[:, :].rearrange("p (kt j) -> p kt j", kt=KT),
                    in_=k.vs[:, :].rearrange("p (kt j) -> p kt j", kt=KT)[
                        :, :, p * 520:(p + 1) * 520]).then_inc(k.s_kv.h, 16)
                k.s_kv.n += 16
                for qb in range(5):
                    if qb < 4:
                        i = 4 * p + qb
                        if i >= 2:
                            sp.wait_ge(k.s_qrl.h, i - 1)
                        k.s_qb2[i % 2].inc(sp.dma_start(
                            out=qbb[qb % 2][:, :].rearrange("p (m t2) -> p m t2", m=4),
                            in_=k.qs[:, :].rearrange("p (m t2) -> p m t2", m=MC)[
                                :, 4 * p:4 * p + 4, qb * 1024:(qb + 1) * 1024]
                        ), 16)
                    if qb >= 1:
                        for jh in range(8):
                            sums_dma(p, qb - 1, jh)

        @blk.tensor
        def _(te):
            ucnt = [0, 0]

            for p in range(2):
                te.wait_ge(k.s_kv.h, 32 * (p + 1))
                for qb in range(4):
                    i = 4 * p + qb
                    te.wait_ge(k.s_qb2[i % 2].h, 16 * (i // 2 + 1))
                    for jp in range(4):
                        hA = 8 * p + 2 * jp
                        hB = hA + 1
                        jm = hA // 2 - 4 * p
                        pi = (p * 4 + qb) * 4 + jp
                        nk = {0: len(KEEP[hA]), 1: len(KEEP[hB])}
                        kts = KEEP[hB]
                        first0 = 32 - nk[0]
                        wid = {0: 0, 1: 0}
                        pend = {0: [], 1: []}

                        def scores_wait(x):
                            u = ucnt[x]
                            if u >= 2:
                                te.wait_ge(k.s_sexX[x].h, u - 1)

                        def scores(x, kt, half):
                            u = ucnt[x]
                            po = 64 * x
                            k.s_smX[x].inc(te.matmul(
                                pss[x][u % 2][:, :],
                                ktb[po:po + 64, jm * S + kt * 128:
                                    jm * S + (kt + 1) * 128],
                                qbb[qb % 2][po:po + 64,
                                            jm * 1024 + half * 512:
                                            jm * 1024 + (half + 1) * 512],
                                start=True, stop=True))
                            pend[x].append((u, kt, half, wid[x]))
                            ucnt[x] += 1
                            wid[x] += 1

                        def av(x):
                            u, kt0, half, widx = pend[x].pop(0)
                            jh0 = 2 * jp + x
                            if widx == 0 and pi >= 1:
                                te.wait_ge(k.s_sml.h, 4 * (pi - 1) + 2 * (x + 1))
                            te.wait_ge(k.s_sexX[x].h, u + 1)
                            k.s_savX[x].inc(te.matmul(
                                pso2[x][0:65, half * 512:(half + 1) * 512],
                                vhb[:, kt0 * 520 + jh0 * 65:
                                    kt0 * 520 + jh0 * 65 + 65],
                                ptb[x][u % 3][:, :],
                                start=(widx <= 1),
                                stop=(widx >= 2 * nk[x] - 2)))

                        for ki, kt in enumerate(kts):
                            for half in range(2):
                                if kt >= first0:
                                    scores_wait(0)
                                scores_wait(1)
                                if kt >= first0:
                                    scores(0, kt, half)
                                scores(1, kt, half)
                                while len(pend[1]) > 2:
                                    av(1)
                                while len(pend[0]) > 2:
                                    av(0)
                        while pend[1]:
                            av(1)
                        while pend[0]:
                            av(0)
                    k.s_qrl.inc(te.nop())
        @blk.scalar
        def _(sc):
            ucnt = [0, 0]
            for p in range(2):
                for qb in range(4):
                    for jp in range(4):
                        hA = 8 * p + 2 * jp
                        nk0 = len(KEEP[hA])
                        first0 = 32 - nk0
                        for kt in KEEP[hA + 1]:
                            for half in range(2):
                                for x in (0, 1):
                                    if x == 0 and kt < first0:
                                        continue
                                    h = hA + x
                                    u = ucnt[x]
                                    sc.wait_ge(k.s_smX[x].h, u + 1)
                                    if u >= 3:
                                        sc.wait_ge(k.s_savX[x].h, u - 2)
                                    sc.activation(
                                        out=ptb[x][u % 3][:, :],
                                        in_=pss[x][u % 2][:, :], func=AF.Exp,
                                        bias=k.absb[:, h * KT + kt:h * KT + kt + 1]
                                    ).then_inc(k.s_sexX[x].h, 1)
                                    k.s_sexX[x].n += 1
                                    ucnt[x] += 1
        @blk.vector
        def _(v):
            tot = [0, 0]
            for p in range(2):
                for qb in range(4):
                    for jp in range(4):
                        hA = 8 * p + 2 * jp
                        m = hA // 2
                        pi = (p * 4 + qb) * 4 + jp
                        for x in (0, 1):
                            h = hA + x
                            hi = 2 * pi + x
                            tot[x] += 2 * len(KEEP[h])
                            v.wait_ge(k.s_savX[x].h, tot[x])
                            if hi >= 2:
                                v.wait_ge(k.s_sd2[hi % 2].h, 16 * (hi // 2))
                            dst = k.rtn[64 * x:64 * x + 64,
                                        m * S + qb * 1024:m * S + (qb + 1) * 1024]
                            with nc.allow_low_precision(reason="bf16 attn out"):
                                k.s_sml.inc(v.tensor_copy(dst, pso2[x][0:64, :]))
                            k.s_sml.inc(v.tensor_copy(sstage[hi % 2][0:1, :],
                                                      pso2[x][64:65, :]))

    if DEBUG:
        s_rtd = Sem("s_rtd")
        with nc.Block("phBdbg") as blk:
            @blk.sync
            def _(sp):
                sp.dma_start(out=k.rts[:, :], in_=k.rtn[:, :]).then_inc(s_rtd.h, 16)
                sp.wait_ge(s_rtd.h, 16)
    stB.close()


def emit_C(k):
    nc = k.nc
    stC = ExitStack()
    sb = lambda name, shape, dt: stC.enter_context(nc.sbuf_tensor(name, shape, dt))
    ps = lambda name, shape, dt: stC.enter_context(nc.psum_tensor(name, shape, dt))

    otc = sb("otc", [128, MC * 512], BF16)
    otok = [sb(f"otok{i}", [128, DM], BF16) for i in range(2)]
    xrb = sb("xrb", [128, 4 * DM], F32)
    arb = sb("arb", [128, 4 * DM], F32)
    o1w = sb("o1w", [128, 4 * DM], F32)
    n0r4 = [sb(f"n0r{i}", [128, DM], F32) for i in range(4)]
    scrC = sb("scrC", [128, DM], F32)
    scrCj = sb("scrCj", [128, DM], BF16)
    n1b = [sb(f"n1b{i}", [128, DM], BF16) for i in range(4)]
    n1tc = sb("n1tc", [128, MC * 512], BF16)
    csum = sb("csum", [128, 4], F32)
    csq = sb("csq", [128, 4], F32)
    cms = sb("cms", [128, 4], F32)
    cvp = sb("cvp", [128, 4], F32)
    crc = sb("crc", [128, 4], F32)
    cr0 = sb("cr0", [128, 4], F32)
    cnw = sb("cnw", [128, 4], F32)

    srp = sb("srp", [16, 512], F32)
    rr2 = sb("rr2", [16, 512], BF16)
    sel16 = sb("sel16", [16, 8 * 128], BF16)
    cvt = [sb(f"cvt{i}", [128, 4096], BF16) for i in range(2)]
    psat = [ps(f"psat{i}", [128, 512], F32) for i in range(4)]
    pct = [ps(f"pct{i}", [128, 1024], BF16) for i in range(2)]
    psbC = [ps(f"psbC{i}", [128, 512], F32) for i in range(2)]

    _mk_sems(k, ["s_xr", "s_omm", "s_oev", "s_otp", "s_ote", "s_ar", "s_csq",
                 "s_crc", "s_o1",
                 "s_o1d", "s_n1", "s_ntp", "s_nte", "s_n1d", "s_srl", "s_rc2", "s_n0r",
                 "s_bc2", "s_nrm", "s_cvl", "s_cvs"])

    with nc.Block("phC") as blk:
        @blk.gpsimd
        def _(g):
            c3 = lambda b, n: cvt[b][:, 0:n].rearrange("p (c f) -> p c f", c=8)
            c4 = lambda b, n: cvt[b][:, 0:n].rearrange(
                "p (e fc d) -> p e fc d", e=8, fc=4)
            pieces = []
            for e in range(E):
                pieces.append((
                    k.wg_d[e, :, :].rearrange("(c p) f -> p c f", p=128),
                    4096, c3, k.wbf_g[e, :, :]))
                pieces.append((
                    k.wu_d[e, :, :].rearrange("(c p) f -> p c f", p=128),
                    4096, c3, k.wbf_u[e, :, :]))
            for f in range(8):
                pieces.append((
                    k.sg_d[:, f * 128:(f + 1) * 128].rearrange(
                        "(c p) f -> p c f", p=128), 1024, c3,
                    k.wbf_sg[f, :, :]))
                pieces.append((
                    k.su_d[:, f * 128:(f + 1) * 128].rearrange(
                        "(c p) f -> p c f", p=128), 1024, c3,
                    k.wbf_su[f, :, :]))
            for dc in range(8):
                pieces.append((
                    k.wd_d.rearrange("e (fc p) d -> p e fc d", p=128)[
                        :, :, :, dc * 128:(dc + 1) * 128], 4096, c4,
                    k.wbf_d[dc, :, :]))
                pieces.append((
                    k.sd_d.rearrange("(fs p) d -> p fs d", p=128)[
                        :, :, dc * 128:(dc + 1) * 128], 1024, c3,
                    k.wbf_sd[dc, :, :]))

            def load(i):
                in_ap, ncols, mk, _ = pieces[i]
                if i >= 2:
                    g.wait_ge(k.s_cvs.h, 16 * (i - 1))
                k.s_cvl.inc(g.dma_start(out=mk(i % 2, ncols), in_=in_ap), 16)

            load(0)
            for i in range(len(pieces)):
                if i + 1 < len(pieces):
                    load(i + 1)
                g.wait_ge(k.s_cvl.h, 16 * (i + 1))
                k.s_cvs.inc(g.dma_start(
                    out=pieces[i][3], in_=cvt[i % 2][:, 0:pieces[i][1]]), 16)

        @blk.sync
        def _(sp):
            sp.wait_ge(k.s_sd2[0].h, k.s_sd2[0].n)
            sp.wait_ge(k.s_sd2[1].h, k.s_sd2[1].n)

            def stores(qc):
                for tt in range(4):
                    t = 4 * qc + tt
                    sp.wait_ge(k.s_o1.h, t + 1)
                    sp.dma_start(out=k.o1s[t * 128:(t + 1) * 128, :],
                                 in_=o1w[:, tt * DM:(tt + 1) * DM]
                                 ).then_inc(k.s_o1d.h, 16)
                    k.s_o1d.n += 16
                sp.wait_ge(k.s_nte.h, 4 * (qc + 1))
                sp.dma_start(
                    out=k.n1s[:, qc * 4096:(qc + 1) * 4096],
                    in_=n1tc[:, :]).then_inc(k.s_n1d.h, 16)
                k.s_n1d.n += 16

            k.s_srl.inc(sp.dma_start(out=sel16[:, :], in_=k.c_sel16[:, :]), 16)

            def srl(j):
                if j >= 1:
                    sp.wait_ge(k.s_rc2.h, j)
                for mc in range(MC):
                    k.s_srl.inc(sp.dma_start(
                        out=srp[2 * mc:2 * mc + 2, :],
                        in_=k.sums_hbm[2 * mc:2 * mc + 2, j * 512:(j + 1) * 512]
                    ), 16)

            srl(0)
            for qc in range(NCH):
                if qc >= 1:
                    sp.wait_ge(k.s_o1.h, 4 * qc)
                sp.dma_start(out=xrb[:, :].rearrange("p (tt d) -> p tt d", tt=4),
                             in_=k.x[qc * 512:(qc + 1) * 512, :].rearrange(
                                 "(tt p) d -> p tt d", p=128)
                             ).then_inc(k.s_xr.h, 16)
                k.s_xr.n += 16
                if qc + 1 < NCH:
                    srl(qc + 1)
                if qc >= 1:
                    stores(qc - 1)
            stores(NCH - 1)

        @blk.tensor
        def _(te):
            for qc in range(NCH):
                te.wait_ge(k.s_rc2.h, qc + 1)
                for mc in range(MC):
                    gmc = 8 * qc + mc
                    if gmc >= 2:
                        te.wait_ge(k.s_nrm.h, gmc - 1)
                    k.s_bc2.inc(te.matmul(
                        psbC[mc % 2][:, :], sel16[0:16, mc * 128:(mc + 1) * 128],
                        rr2[0:16, :],
                        start=True, stop=True))
                te.wait_ge(k.s_nrm.h, 8 * (qc + 1))
                for mo in range(8):
                    gi = 8 * qc + mo
                    if gi >= 4:
                        te.wait_ge(k.s_oev.h, gi - 3)
                    for kc in range(MC):
                        mm = te.matmul(psat[gi % 4][:, :],
                                       k.wot[:, kc * 1024 + mo * 128:
                                             kc * 1024 + (mo + 1) * 128],
                                       k.rtn[:, kc * S + qc * 512:
                                             kc * S + (qc + 1) * 512],
                                       start=(kc == 0), stop=(kc == MC - 1))
                    k.s_omm.inc(mm)
                te.wait_ge(k.s_oev.h, 8 * (qc + 1))
                for tt in range(4):
                    gp = 4 * qc + tt
                    if gp >= 1:
                        te.wait_ge(k.s_ote.h, gp)
                    if gp >= 2:
                        te.wait_ge(k.s_ar.h, gp - 1)
                    for kc in range(MC):
                        k.s_otp.inc(te.transpose(
                            pct[0][:, kc * 128:(kc + 1) * 128],
                            otc[:, kc * 512 + tt * 128:kc * 512 + tt * 128 + 128],
                            k.idbf[:, :]))
                if qc >= 1:
                    for tt in range(4):
                        gp = 4 * (qc - 1) + tt
                        te.wait_ge(k.s_n1.h, gp + 1)
                        if gp >= 1:
                            te.wait_ge(k.s_nte.h, gp)
                        for kc in range(MC):
                            k.s_ntp.inc(te.transpose(
                                pct[1][:, kc * 128:(kc + 1) * 128],
                                n1b[tt][:, kc * 128:(kc + 1) * 128],
                                k.idbf[:, :]))
            for tt in range(4):
                gp = 4 * (NCH - 1) + tt
                te.wait_ge(k.s_n1.h, gp + 1)
                te.wait_ge(k.s_nte.h, gp)
                for kc in range(MC):
                    k.s_ntp.inc(te.transpose(
                        pct[1][:, kc * 128:(kc + 1) * 128],
                        n1b[tt][:, kc * 128:(kc + 1) * 128],
                        k.idbf[:, :]))

        @blk.scalar
        def _(sc):
            for qc in range(NCH):
                for mo in range(8):
                    gi = 8 * qc + mo
                    sc.wait_ge(k.s_omm.h, gi + 1)
                    k.s_oev.inc(sc.activation(
                        out=otc[:, mo * 512:(mo + 1) * 512],
                        in_=psat[gi % 4][:, :], func=AF.Identity,
                        bias=k.biast["o"][:, mo:mo + 1]))
                for tt in range(4):
                    gp = 4 * qc + tt
                    t = gp
                    sc.wait_ge(k.s_otp.h, 8 * (gp + 1))
                    k.s_ote.inc(sc.copy(out=otok[tt % 2][:, :], in_=pct[0][:, :]))
                    sc.wait_ge(k.s_xr.h, 16 * (qc + 1))
                    sc.drain()
                    k.s_n0r.inc(sc.activation(
                        out=n0r4[tt][:, :], in_=xrb[:, tt * DM:(tt + 1) * DM],
                        func=AF.Identity, bias=k.nr0sb[:, t:t + 1],
                        scale=k.r0sb[:, t:t + 1]))
                for tt in range(4):
                    gp = 4 * qc + tt
                    sc.wait_ge(k.s_ar.h, gp + 1)
                    sc.drain()
                    k.s_csq.inc(sc.activation(
                        out=scrCj[:, :], in_=arb[:, tt * DM:(tt + 1) * DM],
                        func=AF.Square, accum_out=csq[:, tt:tt + 1]))
                sc.wait_ge(k.s_crc.h, 4 * qc + 1)
                sc.activation(out=crc[:, :], in_=cvp[:, :], func=AF.Ln)
                sc.drain()
                k.s_crc.inc(sc.activation(out=cnw[:, :], in_=crc[:, :],
                                          func=AF.Exp, scale=-0.5))
                for tt in range(4):
                    gp = 4 * qc + tt
                    sc.wait_ge(k.s_o1.h, gp + 1)
                    sc.drain()
                    k.s_csq.inc(sc.activation(
                        out=scrCj[:, :], in_=o1w[:, tt * DM:(tt + 1) * DM],
                        func=AF.Square, accum_out=csq[:, tt:tt + 1]))
                sc.wait_ge(k.s_crc.h, 4 * qc + 3)
                sc.activation(out=crc[:, :], in_=cvp[:, :], func=AF.Ln)
                sc.drain()
                k.s_crc.inc(sc.activation(out=cnw[:, :], in_=crc[:, :],
                                          func=AF.Exp, scale=-0.5))
                if qc >= 1:
                    if qc >= 2:
                        sc.wait_ge(k.s_n1d.h, 16 * (qc - 1))
                    for tt in range(4):
                        gp = 4 * (qc - 1) + tt
                        sc.wait_ge(k.s_ntp.h, 8 * (gp + 1))
                        k.s_nte.inc(sc.copy(
                            out=n1tc[:, :].rearrange("p (kc t2) -> p kc t2", kc=MC)[
                                :, :, tt * 128:(tt + 1) * 128],
                            in_=pct[1][:, :].rearrange("p (kc t2) -> p kc t2", kc=MC)))
            sc.wait_ge(k.s_n1d.h, 16 * (NCH - 1))
            for tt in range(4):
                gp = 4 * (NCH - 1) + tt
                sc.wait_ge(k.s_ntp.h, 8 * (gp + 1))
                k.s_nte.inc(sc.copy(
                    out=n1tc[:, :].rearrange("p (kc t2) -> p kc t2", kc=MC)[
                        :, :, tt * 128:(tt + 1) * 128],
                    in_=pct[1][:, :].rearrange("p (kc t2) -> p kc t2", kc=MC)))

        @blk.vector
        def _(v):
            def stat_chain(v, sqsem, sqval, rcsem, r0sem, r0val):
                v.wait_ge(sqsem.h, sqval)
                v.drain()
                v.tensor_scalar(cms[:, :], csum[:, :], 1.0 / DM, None,
                                ALU.mult, ALU.bypass)
                v.tensor_scalar(cvp[:, :], csq[:, :], 1.0 / DM, EPS,
                                ALU.mult, ALU.add)
                v.drain()
                v.tensor_tensor(cnw[:, :], cms[:, :], cms[:, :], ALU.mult)
                v.drain()
                rcsem.inc(v.tensor_tensor(cvp[:, :], cvp[:, :], cnw[:, :],
                                          ALU.subtract))
                v.wait_ge(r0sem.h, r0val)

            def recips(qc):
                v.wait_ge(k.s_srl.h, 16 + 128 * (qc + 1))
                with nc.allow_low_precision(reason="bf16 1/sum"):
                    k.s_rc2.inc(v.reciprocal(rr2[:, :], srp[:, :]))

            recips(0)
            for qc in range(NCH):
                for mc in range(MC):
                    gmc = 8 * qc + mc
                    v.wait_ge(k.s_bc2.h, gmc + 1)
                    rsl = k.rtn[:, mc * S + qc * 512:mc * S + (qc + 1) * 512]
                    with nc.allow_low_precision(reason="bf16 attention normalize"):
                        k.s_nrm.inc(v.tensor_tensor(rsl, rsl, psbC[mc % 2][:, :],
                                                    ALU.mult))
                if qc + 1 < NCH:
                    recips(qc + 1)
                for tt in range(4):
                    gp = 4 * qc + tt
                    v.wait_ge(k.s_ote.h, gp + 1)
                    v.wait_ge(k.s_n0r.h, gp + 1)
                    k.s_ar.inc(v.scalar_tensor_tensor(
                        out=arb[:, tt * DM:(tt + 1) * DM],
                        in0=otok[tt % 2][:, :], scalar=k.rzb[:, 0:1],
                        in1=n0r4[tt][:, :], op0=ALU.mult, op1=ALU.add))
                v.drain()
                for tt in range(4):
                    v.reduce_sum(csum[:, tt:tt + 1],
                                 arb[:, tt * DM:(tt + 1) * DM], axis=AX.X)
                stat_chain(v, k.s_csq, 8 * qc + 4, k.s_crc, k.s_crc, 4 * qc + 2)
                if qc >= 1:
                    v.wait_ge(k.s_o1d.h, 64 * qc)
                for tt in range(4):
                    t = 4 * qc + tt
                    v.tensor_scalar(scrC[:, :], arb[:, tt * DM:(tt + 1) * DM],
                                    cms[:, tt:tt + 1], cnw[:, tt:tt + 1],
                                    ALU.subtract, ALU.mult)
                    v.drain()
                    k.s_o1.inc(v.tensor_tensor(o1w[:, tt * DM:(tt + 1) * DM],
                                               scrC[:, :],
                                               xrb[:, tt * DM:(tt + 1) * DM],
                                               ALU.add))
                    v.drain()
                for tt in range(4):
                    v.reduce_sum(csum[:, tt:tt + 1],
                                 o1w[:, tt * DM:(tt + 1) * DM], axis=AX.X)
                stat_chain(v, k.s_csq, 8 * qc + 8, k.s_crc, k.s_crc, 4 * qc + 4)
                for tt in range(4):
                    t = 4 * qc + tt
                    if qc >= 1:
                        v.wait_ge(k.s_ntp.h, 8 * (4 * (qc - 1) + tt + 1))
                    k.s_n1.inc(v.tensor_scalar(
                        n1b[tt][:, :], o1w[:, tt * DM:(tt + 1) * DM],
                        cms[:, tt:tt + 1], cnw[:, tt:tt + 1],
                        ALU.subtract, ALU.mult))

    stC.close()


def emit_D(k):
    nc = k.nc
    stD = ExitStack()
    sb = lambda name, shape, dt: stD.enter_context(nc.sbuf_tensor(name, shape, dt))
    ps = lambda name, shape, dt: stD.enter_context(nc.psum_tensor(name, shape, dt))

    n1c = [sb(f"n1c{i}", [128, MC * 512], BF16) for i in range(2)]
    wrsb = sb("wrsb", [128, 64], BF16)
    wgb = [sb(f"wgb{i}", [128, 4096], BF16) for i in range(2)]
    wub = [sb(f"wub{i}", [128, 4096], BF16) for i in range(2)]
    wde = [sb(f"wde{i}", [128, 4096], BF16) for i in range(4)]
    sde = [sb(f"sde{i}", [128, 1024], BF16) for i in range(4)]
    sgc = [sb(f"sgc{i}", [128, 1024], BF16) for i in range(2)]
    suc = [sb(f"suc{i}", [128, 1024], BF16) for i in range(2)]
    hb = sb("hb", [128, 32 * 512], BF16)
    midb = sb("midb", [128, 8 * 512], BF16)
    lgt = sb("lgt", [8, 512], F32)
    ltok = sb("ltok", [128, 32], F32)
    exb = sb("exb", [128, 32], F32)
    mskb = sb("mskb", [128, 32], F32)
    selb = sb("selb", [128, 32], F32)
    rwn = sb("rwn", [128, 32], F32)
    rwtok = sb("rwtok", [128, 32], F32)
    nm1 = sb("nm1", [128, 4], F32)
    m1s = sb("m1s", [128, 4], F32)
    m2s = sb("m2s", [128, 4], F32)
    zs = sb("zs", [128, 4], F32)
    rzz = sb("rzz", [128, 4], F32)
    rwt = sb("rwt", [8, 512], BF16)
    rwbe = [sb(f"rwbe{i}", [128, 512], BF16) for i in range(2)]
    sgt = [sb(f"sgt{i}", [128, 512], BF16) for i in range(2)]
    ut = [sb(f"ut{i}", [128, 512], BF16) for i in range(2)]
    ffnt = sb("ffnt", [128, 8 * 512], BF16)
    ftokb = sb("ftokb", [128, 4 * DM], F32)
    o1rb = [sb(f"o1rb{i}", [128, DM], F32) for i in range(4)]
    ybuf = [sb(f"ybuf{i}", [128, DM], F32) for i in range(4)]
    fsum = sb("fsum", [128, 4], F32)
    fsq = sb("fsq", [128, 4], F32)
    fms = sb("fms", [128, 4], F32)
    fvp = sb("fvp", [128, 4], F32)
    frcs = sb("frcs", [128, 4], F32)
    fr0s = sb("fr0s", [128, 4], F32)
    fnw = sb("fnw", [128, 4], F32)
    scrD = sb("scrD", [128, DM], BF16)

    psg = [ps(f"psg{i}", [128, 512], F32) for i in range(2)]
    psu = [ps(f"psu{i}", [128, 512], F32) for i in range(2)]
    psr = ps("psr", [128, 512], F32)
    psT = ps("psT", [128, 512], F32)
    psf = [ps(f"psf{i}", [128, 1024], BF16) for i in range(2)]

    _mk_sems(k, ["s_wr", "s_rmm", "s_rev", "s_st",
                 "s_ltp", "s_lte", "s_nm1", "s_rex", "s_rwd", "s_rtp", "s_rte",
                 "s_rbm", "s_rbe", "s_gmm", "s_gsl", "s_uml", "s_hml",
                 "s_dmm", "s_dev", "s_ftp", "s_fte", "s_fsq", "s_frc",
                 "s_yv"])
    Sem = k.Sem
    k.s_n1l2 = [Sem("s_n1l0"), Sem("s_n1l1")]
    k.s_o1l4 = [Sem(f"s_o1l{i}") for i in range(4)]
    k.s_wg2 = [Sem("s_wg0"), Sem("s_wg1")]
    k.s_wd4 = [Sem(f"s_wd{i}") for i in range(4)]
    k.s_sg2 = [Sem("s_sg0"), Sem("s_sg1")]
    k.s_yo4 = [Sem(f"s_yo{i}") for i in range(4)]

    def psg_rel(u):
        T, j = u // 48, u % 48
        if j < 40:
            return (k.s_gsl, 40 * T + j + 1)
        return (k.s_dev, 8 * T + (j - 40) + 1)

    with nc.Block("phD") as blk:
        @blk.sync
        def _(sp):
            sp.wait_ge(k.s_n1d.h, k.s_n1d.n)
            sp.wait_ge(k.s_o1d.h, k.s_o1d.n)
            def ystores(T):
                for tt in range(4):
                    t = 4 * T + tt
                    sp.wait_ge(k.s_yv.h, t + 1)
                    k.s_yo4[tt].inc(sp.dma_start(
                        out=k.y[t * 128:(t + 1) * 128, :],
                        in_=ybuf[tt][:, :]), 16)

            for T in range(NCH):
                if T >= 2:
                    sp.wait_ge(k.s_gmm.h, 80 * (T - 1))
                k.s_n1l2[T % 2].inc(sp.dma_start(
                    out=n1c[T % 2][:, :],
                    in_=k.n1s[:, T * 4096:(T + 1) * 4096]), 16)
                for tt in range(4):
                    t = 4 * T + tt
                    if T >= 1:
                        sp.wait_ge(k.s_yv.h, 4 * (T - 1) + tt + 1)
                    k.s_o1l4[tt].inc(sp.dma_start(
                        out=o1rb[tt][:, :],
                        in_=k.o1s[t * 128:(t + 1) * 128, :]), 16)
                if T >= 1:
                    ystores(T - 1)
            ystores(NCH - 1)
            for tt in range(4):
                sp.wait_ge(k.s_yo4[tt].h, k.s_yo4[tt].n)

        @blk.gpsimd
        def _(g):
            k.s_wr.inc(g.dma_start(
                out=wrsb[:, :].rearrange("p (c e) -> p c e", c=8),
                in_=k.wr_d.rearrange("(c p) e -> p c e", p=128),
                allow_slow_non_contiguous=True), 16)

            def wgwu(T):
                for e in range(E):
                    ge = 8 * T + e
                    if ge >= 2:
                        g.wait_ge(k.s_gmm.h, 2 * (40 * ((ge - 2) // 8)
                                                   + 4 * ((ge - 2) % 8) + 4))
                    if T == 0:
                        g.wait_ge(k.s_cvs.h, 16 * (2 * e + 2))
                    k.s_wg2[ge % 2].inc(g.dma_start(
                        out=wgb[e % 2][:, :], in_=k.wbf_g[e, :, :]), 16)
                    k.s_wg2[ge % 2].inc(g.dma_start(
                        out=wub[e % 2][:, :], in_=k.wbf_u[e, :, :]), 16)

            def sgsu(T):
                for fsc in range(8):
                    gl = 8 * T + fsc
                    if gl >= 2:
                        pT, pf = (gl - 2) // 8, (gl - 2) % 8
                        g.wait_ge(k.s_gmm.h, 2 * (40 * pT + 32 + pf + 1))
                    if T == 0:
                        g.wait_ge(k.s_cvs.h, 16 * (18 + 2 * fsc))
                    k.s_sg2[gl % 2].inc(g.dma_start(
                        out=sgc[fsc % 2][:, :], in_=k.wbf_sg[fsc, :, :]), 16)
                    k.s_sg2[gl % 2].inc(g.dma_start(
                        out=suc[fsc % 2][:, :], in_=k.wbf_su[fsc, :, :]), 16)

            def wdsd(T):
                for dc in range(8):
                    gq = 8 * T + dc
                    if gq >= 4:
                        g.wait_ge(k.s_dmm.h, gq - 3)
                    if T == 0:
                        g.wait_ge(k.s_cvs.h, 16 * (34 + 2 * dc))
                    k.s_wd4[dc % 4].inc(g.dma_start(
                        out=wde[dc % 4][:, :], in_=k.wbf_d[dc, :, :]), 16)
                    k.s_wd4[dc % 4].inc(g.dma_start(
                        out=sde[dc % 4][:, :], in_=k.wbf_sd[dc, :, :]), 16)

            wgwu(0)
            sgsu(0)
            for T in range(NCH):
                wdsd(T)
                if T + 1 < NCH:
                    wgwu(T + 1)
                    sgsu(T + 1)
        @blk.tensor
        def _(te):
            te.wait_ge(k.s_wr.h, 16)

            def router_mm(J):
                te.wait_ge(k.s_n1l2[J % 2].h, 16 * (J // 2 + 1))
                te.wait_ge(k.s_rbe.h, 8 * J)
                for kc in range(MC):
                    mm = te.matmul(psr[0:8, :], wrsb[:, kc * 8:(kc + 1) * 8],
                                   n1c[J % 2][:, kc * 512:(kc + 1) * 512],
                                   start=(kc == 0), stop=(kc == MC - 1))
                k.s_rmm.inc(mm)
                te.wait_ge(k.s_rev.h, J + 1)
                if J >= 1:
                    te.wait_ge(k.s_rte.h, J)
                for tt in range(4):
                    k.s_ltp.inc(te.transpose(psT[:, tt * 8:(tt + 1) * 8],
                                             lgt[0:8, tt * 128:(tt + 1) * 128],
                                             k.idf[0:8, 0:8]))

            def router_rtp(J):
                te.wait_ge(k.s_rwd.h, J + 1)
                te.wait_ge(k.s_lte.h, J + 1)
                for tt in range(4):
                    k.s_rtp.inc(te.transpose(psT[0:8, tt * 128:(tt + 1) * 128],
                                             rwtok[:, tt * 8:(tt + 1) * 8],
                                             k.idf[:, :]))

            router_mm(0)
            router_rtp(0)
            for T in range(NCH):
                te.wait_ge(k.s_rte.h, T + 1)
                for e in range(E):
                    ge = 8 * T + e
                    if ge >= 1:
                        te.wait_ge(k.s_rbe.h, ge)
                    k.s_rbm.inc(te.matmul(psr[:, :],
                                          k.ohsb[0:8, e * 128:(e + 1) * 128],
                                          rwt[0:8, :], start=True, stop=True))
                    te.wait_ge(k.s_wg2[ge % 2].h, 32 * (ge // 2 + 1))
                    for fc in range(4):
                        u = 48 * T + 4 * e + fc
                        if u >= 2:
                            sem, val = psg_rel(u - 2)
                            te.wait_ge(sem.h, val)
                        for kc in range(MC):
                            mm = te.matmul(psg[u % 2][:, :],
                                           wgb[e % 2][:, kc * 512 + fc * 128:
                                                      kc * 512 + (fc + 1) * 128],
                                           n1c[T % 2][:, kc * 512:(kc + 1) * 512],
                                           start=(kc == 0), stop=(kc == MC - 1))
                        k.s_gmm.inc(mm)
                        v = 40 * T + 4 * e + fc
                        if v >= 2:
                            te.wait_ge(k.s_uml.h, v - 1)
                        for kc in range(MC):
                            mm = te.matmul(psu[v % 2][:, :],
                                           wub[e % 2][:, kc * 512 + fc * 128:
                                                      kc * 512 + (fc + 1) * 128],
                                           n1c[T % 2][:, kc * 512:(kc + 1) * 512],
                                           start=(kc == 0), stop=(kc == MC - 1))
                        k.s_gmm.inc(mm)
                for fsc in range(8):
                    gl0 = 8 * T + fsc
                    te.wait_ge(k.s_sg2[gl0 % 2].h, 32 * (gl0 // 2 + 1))
                    u = 48 * T + 32 + fsc
                    sem, val = psg_rel(u - 2)
                    te.wait_ge(sem.h, val)
                    for kc in range(MC):
                        mm = te.matmul(psg[u % 2][:, :],
                                       sgc[fsc % 2][:, kc * 128:(kc + 1) * 128],
                                       n1c[T % 2][:, kc * 512:(kc + 1) * 512],
                                       start=(kc == 0), stop=(kc == MC - 1))
                    k.s_gmm.inc(mm)
                    v = 40 * T + 32 + fsc
                    te.wait_ge(k.s_uml.h, v - 1)
                    for kc in range(MC):
                        mm = te.matmul(psu[v % 2][:, :],
                                       suc[fsc % 2][:, kc * 128:(kc + 1) * 128],
                                       n1c[T % 2][:, kc * 512:(kc + 1) * 512],
                                       start=(kc == 0), stop=(kc == MC - 1))
                    k.s_gmm.inc(mm)
                if T + 1 < NCH:
                    router_mm(T + 1)
                for dc in range(8):
                    gq = 8 * T + dc
                    te.wait_ge(k.s_wd4[dc % 4].h, 64 * T + 32 * (dc // 4 + 1))
                    u = 48 * T + 40 + dc
                    sem, val = psg_rel(u - 2)
                    te.wait_ge(sem.h, val)
                    first = True
                    for e in range(E):
                        for fc in range(4):
                            if dc == 0:
                                te.wait_ge(k.s_hml.h, 40 * T + 4 * e + fc + 1)
                            mm = te.matmul(
                                psg[u % 2][:, :],
                                wde[dc % 4][:, (e * 4 + fc) * 128:
                                            (e * 4 + fc + 1) * 128],
                                hb[:, (e * 4 + fc) * 512:
                                   (e * 4 + fc + 1) * 512],
                                start=first, stop=False)
                            first = False
                    for fsc in range(8):
                        if dc == 0:
                            te.wait_ge(k.s_hml.h, 40 * T + 32 + fsc + 1)
                        mm = te.matmul(
                            psg[u % 2][:, :],
                            sde[dc % 4][:, fsc * 128:(fsc + 1) * 128],
                            midb[:, fsc * 512:(fsc + 1) * 512],
                            start=False, stop=(fsc == 7))
                    k.s_dmm.inc(mm)
                    if dc == 1 and T + 1 < NCH:
                        router_rtp(T + 1)
                te.wait_ge(k.s_dev.h, 8 * (T + 1))
                for tt in range(4):
                    gidx = 4 * T + tt
                    if gidx >= 2:
                        te.wait_ge(k.s_fte.h, gidx - 1)
                    for dc in range(8):
                        k.s_ftp.inc(te.transpose(
                            psf[gidx % 2][:, dc * 128:(dc + 1) * 128],
                            ffnt[:, dc * 512 + tt * 128:
                                 dc * 512 + tt * 128 + 128],
                            k.idbf[:, :]))
        @blk.scalar
        def _(sc):
            def router_acts(J):
                sc.wait_ge(k.s_rmm.h, J + 1)
                k.s_rev.inc(sc.activation(out=lgt[0:8, :], in_=psr[0:8, :],
                                          func=AF.Identity, bias=k.ebsb[0:8, 0:1]))
                sc.wait_ge(k.s_ltp.h, 4 * (J + 1))
                k.s_lte.inc(sc.copy(out=ltok[:, :], in_=psT[:, 0:32]))
                sc.wait_ge(k.s_nm1.h, J + 1)
                for tt in range(4):
                    k.s_rex.inc(sc.activation(
                        out=exb[:, tt * 8:(tt + 1) * 8],
                        in_=ltok[:, tt * 8:(tt + 1) * 8], func=AF.Exp,
                        bias=nm1[:, tt:tt + 1]))

            def router_rte(J):
                sc.wait_ge(k.s_rtp.h, 4 * (J + 1))
                if J >= 1:
                    sc.wait_ge(k.s_rbm.h, 8 * J)
                with nc.allow_low_precision(reason="bf16 routing weights"):
                    k.s_rte.inc(sc.copy(out=rwt[0:8, :], in_=psT[0:8, :]))

            router_acts(0)
            router_rte(0)
            for T in range(NCH):
                for e in range(E):
                    ge = 8 * T + e
                    sc.wait_ge(k.s_rbm.h, ge + 1)
                    if ge >= 2:
                        sc.wait_ge(k.s_uml.h, 40 * ((ge - 2) // 8)
                                   + 4 * ((ge - 2) % 8) + 4)
                    with nc.allow_low_precision(reason="bf16 routing bcast"):
                        k.s_rbe.inc(sc.copy(out=rwbe[e % 2][:, :], in_=psr[:, :]))
                    for fc in range(4):
                        j = 4 * e + fc
                        v = 40 * T + j
                        sc.wait_ge(k.s_gmm.h, 2 * v + 1)
                        if v >= 2:
                            sc.wait_ge(k.s_hml.h, v - 1)
                        k.s_gsl.inc(sc.activation(out=sgt[v % 2][:, :],
                                                  in_=psg[(48 * T + j) % 2][:, :],
                                                  func=AF.Silu))
                for j in range(32, 40):
                    v = 40 * T + j
                    sc.wait_ge(k.s_gmm.h, 2 * v + 1)
                    sc.wait_ge(k.s_hml.h, v - 1)
                    k.s_gsl.inc(sc.activation(out=sgt[v % 2][:, :],
                                              in_=psg[(48 * T + j) % 2][:, :],
                                              func=AF.Silu))
                if T + 1 < NCH:
                    router_acts(T + 1)
                for dc in range(8):
                    sc.wait_ge(k.s_dmm.h, 8 * T + dc + 1)
                    with nc.allow_low_precision(reason="bf16 ffn pre-LN"):
                        k.s_dev.inc(sc.copy(out=ffnt[:, dc * 512:(dc + 1) * 512],
                                            in_=psg[(48 * T + 40 + dc) % 2][:, :]))
                    if dc == 1 and T + 1 < NCH:
                        router_rte(T + 1)
                for tt in range(4):
                    gidx = 4 * T + tt
                    sc.wait_ge(k.s_ftp.h, 8 * (gidx + 1))
                    k.s_fte.inc(sc.copy(
                        out=ftokb[:, tt * DM:(tt + 1) * DM],
                        in_=psf[gidx % 2][:, :]))
                for tt in range(4):
                    sc.drain()
                    k.s_fsq.inc(sc.activation(
                        out=scrD[:, :], in_=ftokb[:, tt * DM:(tt + 1) * DM],
                        func=AF.Square, accum_out=fsq[:, tt:tt + 1]))
                sc.wait_ge(k.s_frc.h, 2 * T + 1)
                sc.activation(out=frcs[:, :], in_=fvp[:, :], func=AF.Ln)
                sc.drain()
                k.s_frc.inc(sc.activation(out=fnw[:, :], in_=frcs[:, :],
                                          func=AF.Exp, scale=-0.5))
        @blk.vector
        def _(v):
            def top2(J):
                v.wait_ge(k.s_lte.h, J + 1)
                for tt in range(4):
                    v.reduce_max(nm1[:, tt:tt + 1], ltok[:, tt * 8:(tt + 1) * 8],
                                 axis=AX.X, negate=True)
                v.drain()
                k.s_nm1.inc(v.tensor_scalar(m1s[:, :], nm1[:, :], -1.0, None,
                                            ALU.mult, ALU.bypass))
                v.drain()
                for tt in range(4):
                    v.tensor_scalar(mskb[:, tt * 8:(tt + 1) * 8],
                                    ltok[:, tt * 8:(tt + 1) * 8],
                                    m1s[:, tt:tt + 1], -1e30, ALU.is_ge, ALU.mult)
                v.drain()
                v.tensor_tensor(mskb[:, :], ltok[:, :], mskb[:, :], ALU.add)
                v.drain()
                for tt in range(4):
                    v.reduce_max(m2s[:, tt:tt + 1], mskb[:, tt * 8:(tt + 1) * 8],
                                 axis=AX.X)
                v.drain()
                for tt in range(4):
                    v.tensor_scalar(selb[:, tt * 8:(tt + 1) * 8],
                                    ltok[:, tt * 8:(tt + 1) * 8],
                                    m2s[:, tt:tt + 1], None, ALU.is_ge, ALU.bypass)
                v.drain()
                v.wait_ge(k.s_rex.h, 4 * (J + 1))
                v.tensor_tensor(rwn[:, :], selb[:, :], exb[:, :], ALU.mult)
                v.drain()
                for tt in range(4):
                    v.reduce_sum(zs[:, tt:tt + 1], rwn[:, tt * 8:(tt + 1) * 8],
                                 axis=AX.X)
                v.drain()
                v.reciprocal(rzz[:, :], zs[:, :])
                v.drain()
                for tt in range(4):
                    v.tensor_scalar(rwtok[:, tt * 8:(tt + 1) * 8],
                                    rwn[:, tt * 8:(tt + 1) * 8],
                                    rzz[:, tt:tt + 1], None, ALU.mult, ALU.bypass)
                k.s_rwd.inc(v.drain())

            top2(0)
            for T in range(NCH):
                for e in range(E):
                    ge = 8 * T + e
                    for fc in range(4):
                        vv = 40 * T + 4 * e + fc
                        v.wait_ge(k.s_gmm.h, 2 * vv + 2)
                        v.wait_ge(k.s_rbe.h, ge + 1)
                        v.tensor_tensor(ut[vv % 2][:, :], psu[vv % 2][:, :],
                                        rwbe[e % 2][:, :], ALU.mult)
                        k.s_uml.inc(v.drain())
                        v.wait_ge(k.s_gsl.h, vv + 1)
                        v.tensor_tensor(hb[:, (4 * e + fc) * 512:
                                           (4 * e + fc + 1) * 512],
                                        sgt[vv % 2][:, :], ut[vv % 2][:, :],
                                        ALU.mult)
                        k.s_hml.inc(v.drain())
                for fsc in range(8):
                    vv = 40 * T + 32 + fsc
                    v.wait_ge(k.s_gmm.h, 2 * vv + 2)
                    v.wait_ge(k.s_gsl.h, vv + 1)
                    v.tensor_tensor(midb[:, fsc * 512:(fsc + 1) * 512],
                                    psu[vv % 2][:, :], sgt[vv % 2][:, :], ALU.mult)
                    k.s_uml.inc(v.drain())
                    k.s_hml.inc(v.nop())
                if T + 1 < NCH:
                    top2(T + 1)
                for tt in range(4):
                    v.wait_ge(k.s_fte.h, 4 * T + tt + 1)
                    v.reduce_sum(fsum[:, tt:tt + 1], ftokb[:, tt * DM:(tt + 1) * DM],
                                 axis=AX.X)
                v.wait_ge(k.s_fsq.h, 4 * (T + 1))
                v.drain()
                v.tensor_scalar(fms[:, :], fsum[:, :], 1.0 / DM, None,
                                ALU.mult, ALU.bypass)
                v.tensor_scalar(fvp[:, :], fsq[:, :], 1.0 / DM, EPS,
                                ALU.mult, ALU.add)
                v.drain()
                v.tensor_tensor(fnw[:, :], fms[:, :], fms[:, :], ALU.mult)
                v.drain()
                k.s_frc.inc(v.tensor_tensor(fvp[:, :], fvp[:, :], fnw[:, :],
                                            ALU.subtract))
                v.wait_ge(k.s_frc.h, 2 * T + 2)
                for tt in range(4):
                    t = 4 * T + tt
                    if T >= 1:
                        v.wait_ge(k.s_yo4[tt].h, 16 * T)
                    v.tensor_scalar(ybuf[tt][:, :], ftokb[:, tt * DM:(tt + 1) * DM],
                                    fms[:, tt:tt + 1], fnw[:, tt:tt + 1],
                                    ALU.subtract, ALU.mult)
                    v.drain()
                    v.wait_ge(k.s_o1l4[tt].h, 16 * (T + 1))
                    k.s_yv.inc(v.tensor_tensor(ybuf[tt][:, :], ybuf[tt][:, :],
                                               o1rb[tt][:, :], ALU.add))

    stD.close()


def kernel(**inputs):
    consts = host_consts()
    nc = build_nc()
    xfull = np.ascontiguousarray(np.asarray(inputs["L0_attn_x"]))
    shared = dict(consts)
    for n in WNAMES:
        shared[f"W{n}"] = np.ascontiguousarray(inputs[f"L0_attn_W{n}"])
        shared[f"b{n}"] = np.ascontiguousarray(inputs[f"L0_attn_b{n}"])
        shared[f"A{n}"] = np.ascontiguousarray(inputs[f"L0_attn_A{n}"])
        shared[f"B{n}"] = np.ascontiguousarray(inputs[f"L0_attn_B{n}"])
    shared["rz"] = np.ascontiguousarray(inputs["L0_attn_rezero"])
    shared["Wr"] = np.ascontiguousarray(inputs["L0_ffn_Wr"])
    shared["Wg"] = np.ascontiguousarray(inputs["L0_ffn_Wg"])
    shared["Wu"] = np.ascontiguousarray(inputs["L0_ffn_Wu"])
    shared["Wd"] = np.ascontiguousarray(inputs["L0_ffn_Wd"])
    shared["Sg"] = np.ascontiguousarray(inputs["L0_ffn_Sg"])
    shared["Su"] = np.ascontiguousarray(inputs["L0_ffn_Su"])
    shared["Sd"] = np.ascontiguousarray(inputs["L0_ffn_Sd"])
    shared["eb"] = np.ascontiguousarray(inputs["L0_ffn_expert_bias"])
    in_maps = [dict(shared, x=np.ascontiguousarray(xfull[b])) for b in range(8)]
    res = run_bass_kernel_spmd(nc, in_maps, core_ids=list(range(8)))
    return np.stack([res.results[b]["y"] for b in range(8)], axis=0)


# version 34 (final)
# speedup vs baseline: 1.1991x; 1.1991x over previous
import numpy as np
import ml_dtypes
from contextlib import ExitStack

import concourse.bass as bass
import concourse.mybir as mybir
from concourse.bass_utils import run_bass_kernel_spmd

F32 = mybir.dt.float32
BF16 = mybir.dt.bfloat16
AF = mybir.ActivationFunctionType
ALU = mybir.AluOpType
AX = mybir.AxisListType

S = 4096
DM = 1024
H = 16
NT = 32
NCH = 8
MC = 8
KT = 32
E = 8
FF = 512
FS = 1024
QSCALE = 0.125
LSCALE = 0.125
CSHIFT = 40.0
EPS = 1e-5
SLOPES = [0.5 ** h for h in range(H)]
KEEP = [[kt for kt in range(KT)
         if SLOPES[h] * (4095.0 - (kt * 128 + 127)) <= 25.0] for h in range(H)]

DEBUG = False
PHASES = "PABCD"
A_NCH = NCH
WNAMES = ("q", "k", "v", "o")


def host_consts():
    eye_bf = np.eye(128, dtype=ml_dtypes.bfloat16)
    eye_f = np.eye(128, dtype=np.float32)
    ab = np.zeros((128, H * KT), np.float32)
    kk = np.arange(128, dtype=np.float64)
    for h in range(H):
        for kt in range(KT):
            ab[:, h * KT + kt] = (SLOPES[h] * ((kt * 128 + kk) - 4095.0)
                                  - CSHIFT).astype(np.float32)
    ones_bf = np.ones((1, 128), ml_dtypes.bfloat16)
    oh = np.zeros((8, E * 128), ml_dtypes.bfloat16)
    for e in range(E):
        oh[e, e * 128:(e + 1) * 128] = 1
    sel2 = np.zeros((2, 128), ml_dtypes.bfloat16)
    sel2[0, 0:64] = 1
    sel2[1, 64:128] = 1
    sel16 = np.zeros((16, 8 * 128), ml_dtypes.bfloat16)
    for mc in range(8):
        sel16[2 * mc, mc * 128:mc * 128 + 64] = 1
        sel16[2 * mc + 1, mc * 128 + 64:mc * 128 + 128] = 1
    return dict(c_idbf=np.ascontiguousarray(eye_bf),
                c_idf=np.ascontiguousarray(eye_f),
                c_ab=np.ascontiguousarray(ab),
                c_ones=np.ascontiguousarray(ones_bf),
                c_oh=np.ascontiguousarray(oh),
                c_sel2=np.ascontiguousarray(sel2),
                c_sel16=np.ascontiguousarray(sel16))


class K:
    pass


def _mk_sems(k, names):
    for nm in names:
        setattr(k, nm, k.Sem(nm))


def build_nc():
    nc = bass.Bass("TRN2", target_bir_lowering=False)
    k = K()
    k.nc = nc
    st = ExitStack()
    k.st = st

    k.x = nc.dram_tensor("x", [S, DM], F32, kind="ExternalInput")
    k.wq = {}
    for n in WNAMES:
        k.wq[n] = dict(
            W=nc.dram_tensor(f"W{n}", [DM, DM], F32, kind="ExternalInput"),
            b=nc.dram_tensor(f"b{n}", [DM], F32, kind="ExternalInput"),
            A=nc.dram_tensor(f"A{n}", [8, DM], F32, kind="ExternalInput"),
            B=nc.dram_tensor(f"B{n}", [DM, 8], F32, kind="ExternalInput"))
    k.rz_d = nc.dram_tensor("rz", [1], F32, kind="ExternalInput")
    k.wr_d = nc.dram_tensor("Wr", [DM, E], F32, kind="ExternalInput")
    k.wg_d = nc.dram_tensor("Wg", [E, DM, FF], F32, kind="ExternalInput")
    k.wu_d = nc.dram_tensor("Wu", [E, DM, FF], F32, kind="ExternalInput")
    k.wd_d = nc.dram_tensor("Wd", [E, FF, DM], F32, kind="ExternalInput")
    k.sg_d = nc.dram_tensor("Sg", [DM, FS], F32, kind="ExternalInput")
    k.su_d = nc.dram_tensor("Su", [DM, FS], F32, kind="ExternalInput")
    k.sd_d = nc.dram_tensor("Sd", [FS, DM], F32, kind="ExternalInput")
    k.eb_d = nc.dram_tensor("eb", [E], F32, kind="ExternalInput")
    k.c_idbf = nc.dram_tensor("c_idbf", [128, 128], BF16, kind="ExternalInput")
    k.c_idf = nc.dram_tensor("c_idf", [128, 128], F32, kind="ExternalInput")
    k.c_ab = nc.dram_tensor("c_ab", [128, H * KT], F32, kind="ExternalInput")
    k.c_ones = nc.dram_tensor("c_ones", [1, 128], BF16, kind="ExternalInput")
    k.c_oh = nc.dram_tensor("c_oh", [8, E * 128], BF16, kind="ExternalInput")
    k.c_sel2 = nc.dram_tensor("c_sel2", [2, 128], BF16, kind="ExternalInput")
    k.c_sel16 = nc.dram_tensor("c_sel16", [16, 8 * 128], BF16, kind="ExternalInput")
    k.sums_hbm = nc.dram_tensor("sums_hbm", [H, S], F32, kind="Internal")
    k.wbf_g = nc.dram_tensor("wbf_g", [E, 128, 4096], BF16, kind="Internal")
    k.wbf_u = nc.dram_tensor("wbf_u", [E, 128, 4096], BF16, kind="Internal")
    k.wbf_d = nc.dram_tensor("wbf_d", [8, 128, 4096], BF16, kind="Internal")
    k.wbf_sd = nc.dram_tensor("wbf_sd", [8, 128, 1024], BF16, kind="Internal")
    k.wbf_sg = nc.dram_tensor("wbf_sg", [8, 128, 1024], BF16, kind="Internal")
    k.wbf_su = nc.dram_tensor("wbf_su", [8, 128, 1024], BF16, kind="Internal")

    sk = "ExternalOutput" if DEBUG else "Internal"
    k.qs = nc.dram_tensor("qs", [128, MC * S], BF16, kind=sk)
    k.ks = nc.dram_tensor("ks", [128, MC * S], BF16, kind=sk)
    k.vs = nc.dram_tensor("vs", [128, KT * H * 65], BF16, kind=sk)
    k.n1s = nc.dram_tensor("n1s", [128, MC * S], BF16, kind=sk)
    k.o1s = nc.dram_tensor("o1s", [S, DM], F32, kind=sk)
    if DEBUG:
        k.rts = nc.dram_tensor("rts", [128, MC * S], BF16, kind="ExternalOutput")
    k.y = nc.dram_tensor("y", [S, DM], F32, kind="ExternalOutput")

    k.idbf = st.enter_context(nc.sbuf_tensor("idbf", [128, 128], BF16))
    k.idf = st.enter_context(nc.sbuf_tensor("idf", [128, 128], F32))
    k.onesb = st.enter_context(nc.sbuf_tensor("onesb", [1, 128], BF16))
    k.ohsb = st.enter_context(nc.sbuf_tensor("ohsb", [8, E * 128], BF16))
    k.sel2 = st.enter_context(nc.sbuf_tensor("sel2", [2, 128], BF16))
    k.biast = {n: st.enter_context(nc.sbuf_tensor(f"bias_{n}", [128, 8], F32))
               for n in WNAMES}
    k.rzb = st.enter_context(nc.sbuf_tensor("rzb", [128, 1], F32))
    k.ebsb = st.enter_context(nc.sbuf_tensor("ebsb", [8, 1], F32))

    class Sem:
        def __init__(self, name):
            self.h = st.enter_context(nc.semaphore(name))
            self.n = 0

        def inc(self, inst, kk=1):
            inst.then_inc(self.h, kk)
            self.n += kk
            return self.n

    k.Sem = Sem
    k.s_out = Sem("s_out")

    stBC = ExitStack()
    k.m0sb = stBC.enter_context(nc.sbuf_tensor("m0sb", [128, NT], F32))
    k.r0sb = stBC.enter_context(nc.sbuf_tensor("r0sb", [128, NT], F32))
    k.nr0sb = stBC.enter_context(nc.sbuf_tensor("nr0sb", [128, NT], F32))
    k.wot = stBC.enter_context(nc.sbuf_tensor("wot", [128, MC * DM], BF16))
    k.absb = stBC.enter_context(nc.sbuf_tensor("absb", [128, H * KT], F32))
    k.stBC = stBC

    emit_PA(k)
    k.rtn = stBC.enter_context(nc.sbuf_tensor("rtn", [128, MC * S], BF16))
    if "A" not in PHASES:
        pass
    if "B" in PHASES:
        emit_B(k)
    if "C" in PHASES:
        emit_C(k)
    stBC.close()
    if "D" in PHASES:
        emit_D(k)
    else:
        with nc.Block("stub") as blk:
            @blk.sync
            def _(sp):
                sp.dma_start(out=k.y[0:128, 0:128], in_=k.idf[:, :]
                             ).then_inc(k.s_out.h, 16)
                k.s_out.n += 16
                sp.wait_ge(k.s_out.h, k.s_out.n)
    st.close()
    return nc


def emit_PA(k):
    nc = k.nc
    Sem = k.Sem
    stA = ExitStack()
    sb = lambda name, shape, dt: stA.enter_context(nc.sbuf_tensor(name, shape, dt))
    ps = lambda name, shape, dt: stA.enter_context(nc.psum_tensor(name, shape, dt))

    wqt = sb("wqt", [128, MC * DM], BF16)
    wkt = sb("wkt", [128, MC * DM], BF16)
    wvt = sb("wvt", [128, MC * DM], BF16)
    WT = {"q": wqt, "k": wkt, "v": wvt, "o": k.wot}
    stP = ExitStack()
    sbP = lambda name, shape, dt: stP.enter_context(nc.sbuf_tensor(name, shape, dt))
    scrA = sb("scrA", [128, DM], BF16)
    n0a = [sb(f"n0a{i}", [128, 4 * DM], BF16) for i in range(2)]
    n0t = [sb(f"n0t{i}", [128, MC * 512], BF16) for i in range(2)]
    qtc = [sb(f"qtc{i}", [128, MC * 512], BF16) for i in range(2)]
    ktc = [sb(f"ktc{i}", [128, MC * 512], BF16) for i in range(2)]
    vcb = [sb(f"vcb{i}", [128, 4 * H * 65], BF16) for i in range(2)]
    vtmp = [sb(f"vtmp{i}", [128, 512], BF16) for i in range(2)]
    ssums = [sb(f"ssums{i}", [128, 4], F32) for i in range(2)]
    ssqs = [sb(f"ssqs{i}", [128, 4], F32) for i in range(2)]
    mst = [sb(f"mst{i}", [128, 4], F32) for i in range(2)]
    vpes = [sb(f"vpes{i}", [128, 4], F32) for i in range(2)]
    recs = [sb(f"recs{i}", [128, 4], F32) for i in range(2)]
    r0s = [sb(f"r0s{i}", [128, 4], F32) for i in range(2)]
    nws = [sb(f"nws{i}", [128, 4], F32) for i in range(2)]

    t1s = sbP("t1s", [128, MC * DM], BF16)
    wld = [sbP(f"wld{i}", [128, MC * DM], BF16) for i in range(2)]
    asb = sbP("asb", [8, DM], BF16)
    bsb = sbP("bsb", [128, 64], BF16)
    btr = sbP("btr", [8, DM], BF16)

    pstn = [ps(f"pstn{i}", [128, 1024], BF16) for i in range(2)]
    pstv = [ps(f"pstv{i}", [128, 1024], BF16) for i in range(2)]
    psq = [ps(f"psq{i}", [128, 512], F32) for i in range(2)]

    _mk_sems(k, ["s_cl", "s_wl", "s_ppb", "s_bte", "s_ppm", "s_t1e", "s_ppv",
                 "s_we", "s_rc", "s_sqa", "s_n0", "s_tpn",
                 "s_ten", "s_tpv", "s_tev", "s_mmq", "s_qe", "s_vt", "s_ms"])
    k.s_xc2 = [Sem(f"s_xc{i}") for i in range(2)]
    k.s_spd2 = [Sem("s_spd0"), Sem("s_spd1")]

    with nc.Block("phP") as blk:
        @blk.sync
        def _(sp):
            for src, dst in ((k.c_idbf, k.idbf), (k.c_idf, k.idf),
                             (k.c_ab, k.absb), (k.c_ones, k.onesb),
                             (k.c_oh, k.ohsb), (k.c_sel2, k.sel2)):
                sp.dma_start(out=dst[:, :], in_=src[:, :]).then_inc(k.s_cl.h, 16)
                k.s_cl.n += 16
            sp.dma_start(out=k.ebsb[:, :],
                         in_=k.eb_d.rearrange("(p one) -> p one", one=1),
                         allow_slow_non_contiguous=True).then_inc(k.s_cl.h, 16)
            k.s_cl.n += 16
            sp.dma_start(out=k.rzb[:, :], in_=bass.AP(k.rz_d, 0, [[0, 128], [1, 1]])
                         ).then_inc(k.s_cl.h, 16)
            k.s_cl.n += 16
            for n in WNAMES:
                sp.dma_start(out=k.biast[n][:, :],
                             in_=k.wq[n]["b"].rearrange("(c p) -> p c", p=128),
                             allow_slow_non_contiguous=True).then_inc(k.s_cl.h, 16)
                k.s_cl.n += 16

        @blk.gpsimd
        def _(g):
            for wi, n in enumerate(WNAMES):
                if wi > 0:
                    g.wait_ge(k.s_ppm.h, 16 * wi)
                    g.wait_ge(k.s_ppb.h, 8 * wi)
                g.dma_start(out=asb[0:8, :], in_=k.wq[n]["A"][:, :]
                            ).then_inc(k.s_wl.h, 16)
                k.s_wl.n += 16
                g.dma_start(out=bsb[:, :].rearrange("p (c r) -> p c r", c=8),
                            in_=k.wq[n]["B"].rearrange("(c p) r -> p c r", p=128),
                            allow_slow_non_contiguous=True).then_inc(k.s_wl.h, 16)
                k.s_wl.n += 16
                if wi >= 2:
                    g.wait_ge(k.s_ppv.h, 64 * (wi - 1))
                g.dma_start(out=wld[wi % 2][:, :].rearrange("p (c d) -> p c d", c=8),
                            in_=k.wq[n]["W"].rearrange("(c p) d -> p c d", p=128)
                            ).then_inc(k.s_wl.h, 16)
                k.s_wl.n += 16

        @blk.tensor
        def _(te):
            te.wait_ge(k.s_cl.h, k.s_cl.n)
            for wi, n in enumerate(WNAMES):
                te.wait_ge(k.s_wl.h, 48 * (wi + 1))
                for c in range(8):
                    gi = 8 * wi + c
                    if gi > 0:
                        te.wait_ge(k.s_bte.h, gi)
                    k.s_ppb.inc(te.transpose(pstn[0][0:8, 0:128],
                                             bsb[:, c * 8:(c + 1) * 8],
                                             k.idbf[:, :]))
                te.wait_ge(k.s_bte.h, 8 * (wi + 1))
                for i in range(16):
                    mcd, half = i // 2, i % 2
                    gi = 16 * wi + i
                    if gi >= 2:
                        te.wait_ge(k.s_t1e.h, gi - 1)
                    k.s_ppm.inc(te.matmul(psq[i % 2][:, :],
                                          asb[0:8, mcd * 128:(mcd + 1) * 128],
                                          btr[0:8, half * 512:(half + 1) * 512],
                                          start=True, stop=True))
                te.wait_ge(k.s_wl.h, 48 * (wi + 1))
                for g in range(8):
                    gl = 8 * wi + g
                    if gl >= 2:
                        te.wait_ge(k.s_we.h, gl - 1)
                    for ncc in range(8):
                        k.s_ppv.inc(te.transpose(
                            pstv[gl % 2][:, ncc * 128:(ncc + 1) * 128],
                            wld[wi % 2][:, ncc * 1024 + g * 128:
                                        ncc * 1024 + g * 128 + 128],
                            k.idbf[:, :]))

        @blk.scalar
        def _(sc):
            for wi, n in enumerate(WNAMES):
                for c in range(8):
                    gi = 8 * wi + c
                    sc.wait_ge(k.s_ppb.h, gi + 1)
                    k.s_bte.inc(sc.copy(out=btr[0:8, c * 128:(c + 1) * 128],
                                        in_=pstn[0][0:8, 0:128]))
                t1scale = LSCALE * (QSCALE if n == "q" else 1.0)
                for i in range(16):
                    mcd, half = i // 2, i % 2
                    gi = 16 * wi + i
                    sc.wait_ge(k.s_ppm.h, gi + 1)
                    if wi >= 1:
                        sc.wait_ge(k.s_we.h, 8 * (wi - 1) + mcd + 1)
                    k.s_t1e.inc(sc.activation(
                        out=t1s[:, mcd * 1024 + half * 512:
                                mcd * 1024 + (half + 1) * 512],
                        in_=psq[i % 2][:, :], func=AF.Copy, scale=t1scale))

        @blk.vector
        def _(v):
            v.wait_ge(k.s_cl.h, k.s_cl.n)
            v.tensor_scalar(k.biast["q"][:, :], k.biast["q"][:, :], QSCALE, None,
                            ALU.mult, ALU.bypass)
            for wi, n in enumerate(WNAMES):
                wscale = QSCALE if n == "q" else 1.0
                for g in range(8):
                    gl = 8 * wi + g
                    v.wait_ge(k.s_ppv.h, 64 * wi + 8 * (g + 1))
                    v.wait_ge(k.s_t1e.h, 16 * wi + 2 * g + 2)
                    k.s_we.inc(v.scalar_tensor_tensor(
                        out=WT[n][:, g * 1024:(g + 1) * 1024],
                        in0=pstv[gl % 2][:, :], scalar=wscale,
                        in1=t1s[:, g * 1024:(g + 1) * 1024],
                        op0=ALU.mult, op1=ALU.add))

    if DEBUG:
        s_wdbg = Sem("s_wdbg")
        wts_d = nc.dram_tensor("wts", [4, 128, MC * DM], BF16, kind="ExternalOutput")
        with nc.Block("phPdbg") as blk:
            @blk.sync
            def _(sp):
                for wi, n in enumerate(WNAMES):
                    sp.dma_start(out=wts_d[wi, :, :], in_=WT[n][:, :]
                                 ).then_inc(s_wdbg.h, 16)
                sp.wait_ge(s_wdbg.h, 64)

    if "A" not in PHASES:
        stA.close()
        return

    stP.close()
    xa2 = [sb(f"xa2_{i}", [128, 4 * DM], F32) for i in range(2)]

    def grp_evt(c, i):
        if i // 8 < 2:
            return (k.s_qe, 16 * c + i + 1)
        return (k.s_vt, 8 * c + (i % 8) + 1)

    with nc.Block("phA") as blk:
        @blk.sync
        def _(sp):
            def spill(cp):
                sp.wait_ge(k.s_qe.h, 16 * (cp + 1))
                sp.wait_ge(k.s_tev.h, 8 * (cp + 1))
                sem = k.s_spd2[cp % 2]
                for srcb, dst in ((qtc, k.qs), (ktc, k.ks)):
                    sem.inc(sp.dma_start(
                        out=dst[:, :].rearrange("p (m t2) -> p m t2", m=MC)[
                            :, :, cp * 512:(cp + 1) * 512],
                        in_=srcb[cp % 2][:, :].rearrange("p (m t2) -> p m t2", m=MC)
                    ), 16)
                sem.inc(sp.dma_start(
                    out=k.vs[:, :].rearrange("p (kt j) -> p kt j", kt=KT)[
                        :, 4 * cp:4 * cp + 4, :],
                    in_=vcb[cp % 2][:, :].rearrange("p (kt j) -> p kt j", kt=4)
                ), 16)

            for c in range(1, A_NCH):
                spill(c - 1)
            spill(A_NCH - 1)

        @blk.gpsimd
        def _(g):
            for c in range(min(2, A_NCH)):
                k.s_xc2[c % 2].inc(g.dma_start(
                    out=xa2[c % 2][:, :].rearrange("p (tt d) -> p tt d", tt=4),
                    in_=k.x[c * 512:(c + 1) * 512, :].rearrange(
                        "(tt p) d -> p tt d", p=128)), 16)
            for c in range(A_NCH):
                if c >= 2:
                    g.wait_ge(k.s_spd2[c % 2].h, 48 * ((c - 2) // 2 + 1))
                k.s_ms.inc(g.memset(vcb[c % 2][:, :], 1.0))
                if c + 2 < A_NCH:
                    g.wait_ge(k.s_n0.h, 4 * (c + 1))
                    g.wait_ge(k.s_sqa.h, 4 * (c + 1))
                    k.s_xc2[c % 2].inc(g.dma_start(
                        out=xa2[c % 2][:, :].rearrange("p (tt d) -> p tt d", tt=4),
                        in_=k.x[(c + 2) * 512:(c + 3) * 512, :].rearrange(
                            "(tt p) d -> p tt d", p=128)), 16)

        @blk.scalar
        def _(sc):
            def squares(c):
                sc.wait_ge(k.s_xc2[c % 2].h, 16 * (c // 2 + 1))
                if c >= 2:
                    sc.wait_ge(k.s_rc.h, 2 * c - 3)
                for tt in range(4):
                    sc.drain()
                    k.s_sqa.inc(sc.activation(
                        out=scrA[:, :], in_=xa2[c % 2][:, tt * DM:(tt + 1) * DM],
                        func=AF.Square, accum_out=ssqs[c % 2][:, tt:tt + 1]))

            def sqrtc(c):
                sc.wait_ge(k.s_rc.h, 2 * c + 1)
                sc.activation(out=recs[c % 2][:, :], in_=vpes[c % 2][:, :],
                              func=AF.Ln)
                sc.drain()
                k.s_rc.inc(sc.activation(out=nws[c % 2][:, :],
                                         in_=recs[c % 2][:, :],
                                         func=AF.Exp, scale=-0.5))

            for c in range(min(2, A_NCH)):
                squares(c)
                sqrtc(c)
            for c in range(A_NCH):
                if c >= 2:
                    sc.wait_ge(k.s_mmq.h, 24 * (c - 1))
                for tt in range(4):
                    t = 4 * c + tt
                    sc.wait_ge(k.s_tpn.h, 8 * (t + 1))
                    k.s_ten.inc(sc.copy(
                        out=n0t[c % 2][:, :].rearrange(
                            "p (kc t2) -> p kc t2", kc=MC)[
                            :, :, tt * 128:(tt + 1) * 128],
                        in_=pstn[t % 2][:, :].rearrange(
                            "p (kc t2) -> p kc t2", kc=MC)))
                if c >= 2:
                    sc.wait_ge(k.s_spd2[c % 2].h, 48 * ((c - 2) // 2 + 1))
                sc.wait_ge(k.s_ms.h, c + 1)
                for i in range(24):
                    w, mc = i // 8, i % 8
                    n = WNAMES[w]
                    sc.wait_ge(k.s_mmq.h, 24 * c + i + 1)
                    if w < 2:
                        dst = (qtc if w == 0 else ktc)[c % 2]
                        k.s_qe.inc(sc.activation(
                            out=dst[:, mc * 512:(mc + 1) * 512],
                            in_=psq[i % 2][:, :], func=AF.Identity,
                            bias=k.biast[n][:, mc:mc + 1]))
                    else:
                        k.s_vt.inc(sc.activation(
                            out=vtmp[mc % 2][:, :],
                            in_=psq[i % 2][:, :], func=AF.Identity,
                            bias=k.biast[n][:, mc:mc + 1]))
                        gv = 8 * c + mc
                        sc.wait_ge(k.s_tpv.h, 4 * (gv + 1))
                        k.s_tev.inc(sc.copy(
                            out=vcb[c % 2][:, :].rearrange(
                                "p (tt hh j) -> p tt hh j", tt=4, hh=H)[
                                :, :, 2 * mc:2 * mc + 2, 0:64],
                            in_=pstv[gv % 2][:, 0:512].rearrange(
                                "p (tt hh j) -> p tt hh j", tt=4, hh=2)))
                    if c + 2 < A_NCH:
                        if i == 7:
                            squares(c + 2)
                        elif i == 19:
                            sqrtc(c + 2)
            del squares, sqrtc

        @blk.vector
        def _(v):
            for c in range(A_NCH):
                if c >= 2:
                    v.wait_ge(k.s_tpn.h, 32 * (c - 1))
                v.wait_ge(k.s_xc2[c % 2].h, 16 * (c // 2 + 1))
                p = c % 2
                for tt in range(4):
                    v.reduce_sum(ssums[p][:, tt:tt + 1],
                                 xa2[c % 2][:, tt * DM:(tt + 1) * DM], axis=AX.X)
                v.wait_ge(k.s_sqa.h, 4 * (c + 1))
                v.drain()
                v.tensor_scalar(mst[p][:, :], ssums[p][:, :], 1.0 / DM, None,
                                ALU.mult, ALU.bypass)
                v.tensor_scalar(vpes[p][:, :], ssqs[p][:, :], 1.0 / DM, EPS,
                                ALU.mult, ALU.add)
                v.drain()
                v.tensor_tensor(nws[p][:, :], mst[p][:, :], mst[p][:, :], ALU.mult)
                v.drain()
                k.s_rc.inc(v.tensor_tensor(vpes[p][:, :], vpes[p][:, :],
                                           nws[p][:, :], ALU.subtract))
                v.wait_ge(k.s_rc.h, 2 * c + 2)
                v.tensor_copy(k.m0sb[:, 4 * c:4 * c + 4], mst[p][:, :])
                v.tensor_copy(k.r0sb[:, 4 * c:4 * c + 4], nws[p][:, :])
                v.scalar_tensor_tensor(out=k.nr0sb[:, 4 * c:4 * c + 4],
                                       in0=mst[p][:, :], scalar=-1.0,
                                       in1=nws[p][:, :],
                                       op0=ALU.mult, op1=ALU.mult)
                v.drain()
                for tt in range(4):
                    t = 4 * c + tt
                    k.s_n0.inc(v.tensor_scalar(
                        n0a[c % 2][:, tt * DM:(tt + 1) * DM],
                        xa2[c % 2][:, tt * DM:(tt + 1) * DM],
                        mst[p][:, tt:tt + 1], nws[p][:, tt:tt + 1],
                        ALU.subtract, ALU.mult))

        @blk.tensor
        def _(te):
            for c in range(A_NCH):
                for tt in range(4):
                    t = 4 * c + tt
                    te.wait_ge(k.s_n0.h, t + 1)
                    if t >= 2:
                        te.wait_ge(k.s_ten.h, t - 1)
                    for kc in range(MC):
                        k.s_tpn.inc(te.transpose(
                            pstn[t % 2][:, kc * 128:(kc + 1) * 128],
                            n0a[c % 2][:, tt * DM + kc * 128:tt * DM + kc * 128 + 128],
                            k.idbf[:, :]))
                te.wait_ge(k.s_ten.h, 4 * (c + 1))
                for i in range(24):
                    w, mc = i // 8, i % 8
                    gi = 24 * c + i
                    if gi >= 2:
                        pc, pii = (gi - 2) // 24, (gi - 2) % 24
                        sem, val = grp_evt(pc, pii)
                        te.wait_ge(sem.h, val)
                    wtt = WT[WNAMES[w]]
                    for kc in range(MC):
                        mm = te.matmul(psq[gi % 2][:, :],
                                       wtt[:, kc * 1024 + mc * 128:
                                           kc * 1024 + (mc + 1) * 128],
                                       n0t[c % 2][:, kc * 512:(kc + 1) * 512],
                                       start=(kc == 0), stop=(kc == MC - 1))
                    k.s_mmq.inc(mm)
                    if w == 2:
                        te.wait_ge(k.s_vt.h, 8 * c + mc + 1)
                        gv = 8 * c + mc
                        if gv >= 2:
                            te.wait_ge(k.s_tev.h, gv - 1)
                        for tt in range(4):
                            vi = 32 * c + mc * 4 + tt
                            k.s_tpv.inc(te.transpose(
                                pstv[gv % 2][:, (vi % 4) * 128:(vi % 4) * 128 + 128],
                                vtmp[mc % 2][:, tt * 128:(tt + 1) * 128],
                                k.idbf[:, :]))

    stA.close()


def emit_B(k):
    nc = k.nc
    Sem = k.Sem
    stB = ExitStack()
    sb = lambda name, shape, dt: stB.enter_context(nc.sbuf_tensor(name, shape, dt))
    ps = lambda name, shape, dt: stB.enter_context(nc.psum_tensor(name, shape, dt))

    ktb = sb("ktb", [128, 4 * S], BF16)
    vhb = sb("vhb", [128, KT * 520], BF16)
    qbb = [sb(f"qbb{i}", [128, 4 * 1024], BF16) for i in range(2)]
    ptb = [[sb(f"ptb{x}{i}", [128, 512], BF16) for i in range(3)]
           for x in "AB"]
    sstage = [sb(f"sstage{i}", [1, 1024], F32) for i in range(2)]
    pss = [[ps(f"pss{x}{i}", [128, 512], F32) for i in range(2)] for x in "AB"]
    pso2 = [ps(f"pso{x}", [128, 1024], F32) for x in "AB"]

    _mk_sems(k, ["s_kv", "s_qrl", "s_sml"])
    k.s_qb2 = [Sem("s_qb0"), Sem("s_qb1")]
    k.s_sd2 = [Sem("s_sd0"), Sem("s_sd1")]
    k.s_smX = [Sem("s_smA"), Sem("s_smB")]
    k.s_sexX = [Sem("s_sexA"), Sem("s_sexB")]
    k.s_savX = [Sem("s_savA"), Sem("s_savB")]

    with nc.Block("phB") as blk:
        @blk.sync
        def _(sp):
            sp.wait_ge(k.s_spd2[0].h, k.s_spd2[0].n)
            sp.wait_ge(k.s_spd2[1].h, k.s_spd2[1].n)
            def sums_dma(p, qb, jh):
                hi = (p * 4 + qb) * 8 + jh
                h = 8 * p + jh
                sp.wait_ge(k.s_sml.h, 2 * hi + 2)
                k.s_sd2[hi % 2].inc(sp.dma_start(
                    out=k.sums_hbm[h, qb * 1024:(qb + 1) * 1024],
                    in_=sstage[hi % 2][0:1, :]), 16)
            for p in range(2):
                if p == 1:
                    sp.wait_ge(k.s_sml.h, 2 * 32)
                sp.dma_start(
                    out=ktb[:, :].rearrange("p (m t2) -> p m t2", m=4),
                    in_=k.ks[:, :].rearrange("p (m t2) -> p m t2", m=MC)[
                        :, 4 * p:4 * p + 4, :]).then_inc(k.s_kv.h, 16)
                k.s_kv.n += 16
                sp.dma_start(
                    out=vhb[:, :].rearrange("p (kt j) -> p kt j", kt=KT),
                    in_=k.vs[:, :].rearrange("p (kt j) -> p kt j", kt=KT)[
                        :, :, p * 520:(p + 1) * 520]).then_inc(k.s_kv.h, 16)
                k.s_kv.n += 16
                for qb in range(5):
                    if qb < 4:
                        i = 4 * p + qb
                        if i >= 2:
                            sp.wait_ge(k.s_qrl.h, i - 1)
                        k.s_qb2[i % 2].inc(sp.dma_start(
                            out=qbb[qb % 2][:, :].rearrange("p (m t2) -> p m t2", m=4),
                            in_=k.qs[:, :].rearrange("p (m t2) -> p m t2", m=MC)[
                                :, 4 * p:4 * p + 4, qb * 1024:(qb + 1) * 1024]
                        ), 16)
                    if qb >= 1:
                        for jh in range(8):
                            sums_dma(p, qb - 1, jh)

        @blk.tensor
        def _(te):
            ucnt = [0, 0]

            for p in range(2):
                te.wait_ge(k.s_kv.h, 32 * (p + 1))
                for qb in range(4):
                    i = 4 * p + qb
                    te.wait_ge(k.s_qb2[i % 2].h, 16 * (i // 2 + 1))
                    for jp in range(4):
                        hA = 8 * p + 2 * jp
                        hB = hA + 1
                        jm = hA // 2 - 4 * p
                        pi = (p * 4 + qb) * 4 + jp
                        nk = {0: len(KEEP[hA]), 1: len(KEEP[hB])}
                        kts = KEEP[hB]
                        first0 = 32 - nk[0]
                        wid = {0: 0, 1: 0}
                        pend = {0: [], 1: []}

                        def scores_wait(x):
                            u = ucnt[x]
                            if u >= 2:
                                te.wait_ge(k.s_sexX[x].h, u - 1)

                        def scores(x, kt, half):
                            u = ucnt[x]
                            po = 64 * x
                            k.s_smX[x].inc(te.matmul(
                                pss[x][u % 2][:, :],
                                ktb[po:po + 64, jm * S + kt * 128:
                                    jm * S + (kt + 1) * 128],
                                qbb[qb % 2][po:po + 64,
                                            jm * 1024 + half * 512:
                                            jm * 1024 + (half + 1) * 512],
                                start=True, stop=True))
                            pend[x].append((u, kt, half, wid[x]))
                            ucnt[x] += 1
                            wid[x] += 1

                        def av(x):
                            u, kt0, half, widx = pend[x].pop(0)
                            jh0 = 2 * jp + x
                            if widx == 0 and pi >= 1:
                                te.wait_ge(k.s_sml.h, 4 * (pi - 1) + 2 * (x + 1))
                            te.wait_ge(k.s_sexX[x].h, u + 1)
                            k.s_savX[x].inc(te.matmul(
                                pso2[x][0:65, half * 512:(half + 1) * 512],
                                vhb[:, kt0 * 520 + jh0 * 65:
                                    kt0 * 520 + jh0 * 65 + 65],
                                ptb[x][u % 3][:, :],
                                start=(widx <= 1),
                                stop=(widx >= 2 * nk[x] - 2)))

                        for ki, kt in enumerate(kts):
                            for half in range(2):
                                if kt >= first0:
                                    scores_wait(0)
                                scores_wait(1)
                                if kt >= first0:
                                    scores(0, kt, half)
                                scores(1, kt, half)
                                while len(pend[1]) > 2:
                                    av(1)
                                while len(pend[0]) > 2:
                                    av(0)
                        while pend[1]:
                            av(1)
                        while pend[0]:
                            av(0)
                    k.s_qrl.inc(te.nop())
        @blk.scalar
        def _(sc):
            ucnt = [0, 0]
            for p in range(2):
                for qb in range(4):
                    for jp in range(4):
                        hA = 8 * p + 2 * jp
                        nk0 = len(KEEP[hA])
                        first0 = 32 - nk0
                        for kt in KEEP[hA + 1]:
                            for half in range(2):
                                for x in (0, 1):
                                    if x == 0 and kt < first0:
                                        continue
                                    h = hA + x
                                    u = ucnt[x]
                                    sc.wait_ge(k.s_smX[x].h, u + 1)
                                    if u >= 3:
                                        sc.wait_ge(k.s_savX[x].h, u - 2)
                                    sc.activation(
                                        out=ptb[x][u % 3][:, :],
                                        in_=pss[x][u % 2][:, :], func=AF.Exp,
                                        bias=k.absb[:, h * KT + kt:h * KT + kt + 1]
                                    ).then_inc(k.s_sexX[x].h, 1)
                                    k.s_sexX[x].n += 1
                                    ucnt[x] += 1
        @blk.vector
        def _(v):
            tot = [0, 0]
            for p in range(2):
                for qb in range(4):
                    for jp in range(4):
                        hA = 8 * p + 2 * jp
                        m = hA // 2
                        pi = (p * 4 + qb) * 4 + jp
                        for x in (0, 1):
                            h = hA + x
                            hi = 2 * pi + x
                            tot[x] += 2 * len(KEEP[h])
                            v.wait_ge(k.s_savX[x].h, tot[x])
                            if hi >= 2:
                                v.wait_ge(k.s_sd2[hi % 2].h, 16 * (hi // 2))
                            dst = k.rtn[64 * x:64 * x + 64,
                                        m * S + qb * 1024:m * S + (qb + 1) * 1024]
                            with nc.allow_low_precision(reason="bf16 attn out"):
                                k.s_sml.inc(v.tensor_copy(dst, pso2[x][0:64, :]))
                            k.s_sml.inc(v.tensor_copy(sstage[hi % 2][0:1, :],
                                                      pso2[x][64:65, :]))

    if DEBUG:
        s_rtd = Sem("s_rtd")
        with nc.Block("phBdbg") as blk:
            @blk.sync
            def _(sp):
                sp.dma_start(out=k.rts[:, :], in_=k.rtn[:, :]).then_inc(s_rtd.h, 16)
                sp.wait_ge(s_rtd.h, 16)
    stB.close()


def emit_C(k):
    nc = k.nc
    stC = ExitStack()
    sb = lambda name, shape, dt: stC.enter_context(nc.sbuf_tensor(name, shape, dt))
    ps = lambda name, shape, dt: stC.enter_context(nc.psum_tensor(name, shape, dt))

    otc = sb("otc", [128, MC * 512], BF16)
    otok = [sb(f"otok{i}", [128, DM], BF16) for i in range(2)]
    xrb = sb("xrb", [128, 4 * DM], F32)
    arb = sb("arb", [128, 4 * DM], F32)
    o1w = sb("o1w", [128, 4 * DM], F32)
    n0r4 = [sb(f"n0r{i}", [128, DM], F32) for i in range(4)]
    scrC = sb("scrC", [128, DM], F32)
    scrCj = sb("scrCj", [128, DM], BF16)
    n1b = [sb(f"n1b{i}", [128, DM], BF16) for i in range(4)]
    n1tc = sb("n1tc", [128, MC * 512], BF16)
    csum = sb("csum", [128, 4], F32)
    csq = sb("csq", [128, 4], F32)
    cms = sb("cms", [128, 4], F32)
    cvp = sb("cvp", [128, 4], F32)
    crc = sb("crc", [128, 4], F32)
    cr0 = sb("cr0", [128, 4], F32)
    cnw = sb("cnw", [128, 4], F32)

    srp = sb("srp", [16, 512], F32)
    rr2 = sb("rr2", [16, 512], BF16)
    sel16 = sb("sel16", [16, 8 * 128], BF16)
    cvt = [sb(f"cvt{i}", [128, 4096], BF16) for i in range(2)]
    psat = [ps(f"psat{i}", [128, 512], F32) for i in range(4)]
    pct = [ps(f"pct{i}", [128, 1024], BF16) for i in range(2)]
    psbC = [ps(f"psbC{i}", [128, 512], F32) for i in range(2)]

    _mk_sems(k, ["s_xr", "s_omm", "s_oev", "s_otp", "s_ote", "s_ar", "s_csq",
                 "s_crc", "s_o1",
                 "s_o1d", "s_n1", "s_ntp", "s_nte", "s_n1d", "s_srl", "s_rc2", "s_n0r",
                 "s_bc2", "s_nrm", "s_cvl", "s_cvs"])

    with nc.Block("phC") as blk:
        @blk.gpsimd
        def _(g):
            c3 = lambda b, n: cvt[b][:, 0:n].rearrange("p (c f) -> p c f", c=8)
            c4 = lambda b, n: cvt[b][:, 0:n].rearrange(
                "p (e fc d) -> p e fc d", e=8, fc=4)
            pieces = []
            for e in range(E):
                pieces.append((
                    k.wg_d[e, :, :].rearrange("(c p) f -> p c f", p=128),
                    4096, c3, k.wbf_g[e, :, :]))
                pieces.append((
                    k.wu_d[e, :, :].rearrange("(c p) f -> p c f", p=128),
                    4096, c3, k.wbf_u[e, :, :]))
            for f in range(8):
                pieces.append((
                    k.sg_d[:, f * 128:(f + 1) * 128].rearrange(
                        "(c p) f -> p c f", p=128), 1024, c3,
                    k.wbf_sg[f, :, :]))
                pieces.append((
                    k.su_d[:, f * 128:(f + 1) * 128].rearrange(
                        "(c p) f -> p c f", p=128), 1024, c3,
                    k.wbf_su[f, :, :]))
            for dc in range(8):
                pieces.append((
                    k.wd_d.rearrange("e (fc p) d -> p e fc d", p=128)[
                        :, :, :, dc * 128:(dc + 1) * 128], 4096, c4,
                    k.wbf_d[dc, :, :]))
                pieces.append((
                    k.sd_d.rearrange("(fs p) d -> p fs d", p=128)[
                        :, :, dc * 128:(dc + 1) * 128], 1024, c3,
                    k.wbf_sd[dc, :, :]))

            def load(i):
                in_ap, ncols, mk, _ = pieces[i]
                if i >= 2:
                    g.wait_ge(k.s_cvs.h, 16 * (i - 1))
                k.s_cvl.inc(g.dma_start(out=mk(i % 2, ncols), in_=in_ap), 16)

            load(0)
            for i in range(len(pieces)):
                if i + 1 < len(pieces):
                    load(i + 1)
                g.wait_ge(k.s_cvl.h, 16 * (i + 1))
                k.s_cvs.inc(g.dma_start(
                    out=pieces[i][3], in_=cvt[i % 2][:, 0:pieces[i][1]]), 16)

        @blk.sync
        def _(sp):
            sp.wait_ge(k.s_sd2[0].h, k.s_sd2[0].n)
            sp.wait_ge(k.s_sd2[1].h, k.s_sd2[1].n)

            def stores(qc):
                for tt in range(4):
                    t = 4 * qc + tt
                    sp.wait_ge(k.s_o1.h, t + 1)
                    sp.dma_start(out=k.o1s[t * 128:(t + 1) * 128, :],
                                 in_=o1w[:, tt * DM:(tt + 1) * DM]
                                 ).then_inc(k.s_o1d.h, 16)
                    k.s_o1d.n += 16
                sp.wait_ge(k.s_nte.h, 4 * (qc + 1))
                sp.dma_start(
                    out=k.n1s[:, qc * 4096:(qc + 1) * 4096],
                    in_=n1tc[:, :]).then_inc(k.s_n1d.h, 16)
                k.s_n1d.n += 16

            k.s_srl.inc(sp.dma_start(out=sel16[:, :], in_=k.c_sel16[:, :]), 16)

            def srl(j):
                if j >= 1:
                    sp.wait_ge(k.s_rc2.h, j)
                for mc in range(MC):
                    k.s_srl.inc(sp.dma_start(
                        out=srp[2 * mc:2 * mc + 2, :],
                        in_=k.sums_hbm[2 * mc:2 * mc + 2, j * 512:(j + 1) * 512]
                    ), 16)

            srl(0)
            for qc in range(NCH):
                if qc >= 1:
                    sp.wait_ge(k.s_o1.h, 4 * qc)
                sp.dma_start(out=xrb[:, :].rearrange("p (tt d) -> p tt d", tt=4),
                             in_=k.x[qc * 512:(qc + 1) * 512, :].rearrange(
                                 "(tt p) d -> p tt d", p=128)
                             ).then_inc(k.s_xr.h, 16)
                k.s_xr.n += 16
                if qc + 1 < NCH:
                    srl(qc + 1)
                if qc >= 1:
                    stores(qc - 1)
            stores(NCH - 1)

        @blk.tensor
        def _(te):
            for qc in range(NCH):
                te.wait_ge(k.s_rc2.h, qc + 1)
                for mc in range(MC):
                    gmc = 8 * qc + mc
                    if gmc >= 2:
                        te.wait_ge(k.s_nrm.h, gmc - 1)
                    k.s_bc2.inc(te.matmul(
                        psbC[mc % 2][:, :], sel16[0:16, mc * 128:(mc + 1) * 128],
                        rr2[0:16, :],
                        start=True, stop=True))
                te.wait_ge(k.s_nrm.h, 8 * (qc + 1))
                for mo in range(8):
                    gi = 8 * qc + mo
                    if gi >= 4:
                        te.wait_ge(k.s_oev.h, gi - 3)
                    for kc in range(MC):
                        mm = te.matmul(psat[gi % 4][:, :],
                                       k.wot[:, kc * 1024 + mo * 128:
                                             kc * 1024 + (mo + 1) * 128],
                                       k.rtn[:, kc * S + qc * 512:
                                             kc * S + (qc + 1) * 512],
                                       start=(kc == 0), stop=(kc == MC - 1))
                    k.s_omm.inc(mm)
                te.wait_ge(k.s_oev.h, 8 * (qc + 1))
                for tt in range(4):
                    gp = 4 * qc + tt
                    if gp >= 1:
                        te.wait_ge(k.s_ote.h, gp)
                    if gp >= 2:
                        te.wait_ge(k.s_ar.h, gp - 1)
                    for kc in range(MC):
                        k.s_otp.inc(te.transpose(
                            pct[0][:, kc * 128:(kc + 1) * 128],
                            otc[:, kc * 512 + tt * 128:kc * 512 + tt * 128 + 128],
                            k.idbf[:, :]))
                if qc >= 1:
                    for tt in range(4):
                        gp = 4 * (qc - 1) + tt
                        te.wait_ge(k.s_n1.h, gp + 1)
                        if gp >= 1:
                            te.wait_ge(k.s_nte.h, gp)
                        for kc in range(MC):
                            k.s_ntp.inc(te.transpose(
                                pct[1][:, kc * 128:(kc + 1) * 128],
                                n1b[tt][:, kc * 128:(kc + 1) * 128],
                                k.idbf[:, :]))
            for tt in range(4):
                gp = 4 * (NCH - 1) + tt
                te.wait_ge(k.s_n1.h, gp + 1)
                te.wait_ge(k.s_nte.h, gp)
                for kc in range(MC):
                    k.s_ntp.inc(te.transpose(
                        pct[1][:, kc * 128:(kc + 1) * 128],
                        n1b[tt][:, kc * 128:(kc + 1) * 128],
                        k.idbf[:, :]))

        @blk.scalar
        def _(sc):
            for qc in range(NCH):
                for mo in range(8):
                    gi = 8 * qc + mo
                    sc.wait_ge(k.s_omm.h, gi + 1)
                    k.s_oev.inc(sc.activation(
                        out=otc[:, mo * 512:(mo + 1) * 512],
                        in_=psat[gi % 4][:, :], func=AF.Identity,
                        bias=k.biast["o"][:, mo:mo + 1]))
                for tt in range(4):
                    gp = 4 * qc + tt
                    t = gp
                    sc.wait_ge(k.s_otp.h, 8 * (gp + 1))
                    k.s_ote.inc(sc.copy(out=otok[tt % 2][:, :], in_=pct[0][:, :]))
                    sc.wait_ge(k.s_xr.h, 16 * (qc + 1))
                    sc.drain()
                    k.s_n0r.inc(sc.activation(
                        out=n0r4[tt][:, :], in_=xrb[:, tt * DM:(tt + 1) * DM],
                        func=AF.Identity, bias=k.nr0sb[:, t:t + 1],
                        scale=k.r0sb[:, t:t + 1]))
                for tt in range(4):
                    gp = 4 * qc + tt
                    sc.wait_ge(k.s_ar.h, gp + 1)
                    sc.drain()
                    k.s_csq.inc(sc.activation(
                        out=scrCj[:, :], in_=arb[:, tt * DM:(tt + 1) * DM],
                        func=AF.Square, accum_out=csq[:, tt:tt + 1]))
                sc.wait_ge(k.s_crc.h, 4 * qc + 1)
                sc.activation(out=crc[:, :], in_=cvp[:, :], func=AF.Ln)
                sc.drain()
                k.s_crc.inc(sc.activation(out=cnw[:, :], in_=crc[:, :],
                                          func=AF.Exp, scale=-0.5))
                for tt in range(4):
                    gp = 4 * qc + tt
                    sc.wait_ge(k.s_o1.h, gp + 1)
                    sc.drain()
                    k.s_csq.inc(sc.activation(
                        out=scrCj[:, :], in_=o1w[:, tt * DM:(tt + 1) * DM],
                        func=AF.Square, accum_out=csq[:, tt:tt + 1]))
                sc.wait_ge(k.s_crc.h, 4 * qc + 3)
                sc.activation(out=crc[:, :], in_=cvp[:, :], func=AF.Ln)
                sc.drain()
                k.s_crc.inc(sc.activation(out=cnw[:, :], in_=crc[:, :],
                                          func=AF.Exp, scale=-0.5))
                if qc >= 1:
                    if qc >= 2:
                        sc.wait_ge(k.s_n1d.h, 16 * (qc - 1))
                    for tt in range(4):
                        gp = 4 * (qc - 1) + tt
                        sc.wait_ge(k.s_ntp.h, 8 * (gp + 1))
                        k.s_nte.inc(sc.copy(
                            out=n1tc[:, :].rearrange("p (kc t2) -> p kc t2", kc=MC)[
                                :, :, tt * 128:(tt + 1) * 128],
                            in_=pct[1][:, :].rearrange("p (kc t2) -> p kc t2", kc=MC)))
            sc.wait_ge(k.s_n1d.h, 16 * (NCH - 1))
            for tt in range(4):
                gp = 4 * (NCH - 1) + tt
                sc.wait_ge(k.s_ntp.h, 8 * (gp + 1))
                k.s_nte.inc(sc.copy(
                    out=n1tc[:, :].rearrange("p (kc t2) -> p kc t2", kc=MC)[
                        :, :, tt * 128:(tt + 1) * 128],
                    in_=pct[1][:, :].rearrange("p (kc t2) -> p kc t2", kc=MC)))

        @blk.vector
        def _(v):
            def stat_chain(v, sqsem, sqval, rcsem, r0sem, r0val):
                v.wait_ge(sqsem.h, sqval)
                v.drain()
                v.tensor_scalar(cms[:, :], csum[:, :], 1.0 / DM, None,
                                ALU.mult, ALU.bypass)
                v.tensor_scalar(cvp[:, :], csq[:, :], 1.0 / DM, EPS,
                                ALU.mult, ALU.add)
                v.drain()
                v.tensor_tensor(cnw[:, :], cms[:, :], cms[:, :], ALU.mult)
                v.drain()
                rcsem.inc(v.tensor_tensor(cvp[:, :], cvp[:, :], cnw[:, :],
                                          ALU.subtract))
                v.wait_ge(r0sem.h, r0val)

            def recips(qc):
                v.wait_ge(k.s_srl.h, 16 + 128 * (qc + 1))
                with nc.allow_low_precision(reason="bf16 1/sum"):
                    k.s_rc2.inc(v.reciprocal(rr2[:, :], srp[:, :]))

            recips(0)
            for qc in range(NCH):
                for mc in range(MC):
                    gmc = 8 * qc + mc
                    v.wait_ge(k.s_bc2.h, gmc + 1)
                    rsl = k.rtn[:, mc * S + qc * 512:mc * S + (qc + 1) * 512]
                    with nc.allow_low_precision(reason="bf16 attention normalize"):
                        k.s_nrm.inc(v.tensor_tensor(rsl, rsl, psbC[mc % 2][:, :],
                                                    ALU.mult))
                if qc + 1 < NCH:
                    recips(qc + 1)
                for tt in range(4):
                    gp = 4 * qc + tt
                    v.wait_ge(k.s_ote.h, gp + 1)
                    v.wait_ge(k.s_n0r.h, gp + 1)
                    k.s_ar.inc(v.scalar_tensor_tensor(
                        out=arb[:, tt * DM:(tt + 1) * DM],
                        in0=otok[tt % 2][:, :], scalar=k.rzb[:, 0:1],
                        in1=n0r4[tt][:, :], op0=ALU.mult, op1=ALU.add))
                v.drain()
                for tt in range(4):
                    v.reduce_sum(csum[:, tt:tt + 1],
                                 arb[:, tt * DM:(tt + 1) * DM], axis=AX.X)
                stat_chain(v, k.s_csq, 8 * qc + 4, k.s_crc, k.s_crc, 4 * qc + 2)
                if qc >= 1:
                    v.wait_ge(k.s_o1d.h, 64 * qc)
                for tt in range(4):
                    t = 4 * qc + tt
                    v.tensor_scalar(scrC[:, :], arb[:, tt * DM:(tt + 1) * DM],
                                    cms[:, tt:tt + 1], cnw[:, tt:tt + 1],
                                    ALU.subtract, ALU.mult)
                    v.drain()
                    k.s_o1.inc(v.tensor_tensor(o1w[:, tt * DM:(tt + 1) * DM],
                                               scrC[:, :],
                                               xrb[:, tt * DM:(tt + 1) * DM],
                                               ALU.add))
                    v.drain()
                for tt in range(4):
                    v.reduce_sum(csum[:, tt:tt + 1],
                                 o1w[:, tt * DM:(tt + 1) * DM], axis=AX.X)
                stat_chain(v, k.s_csq, 8 * qc + 8, k.s_crc, k.s_crc, 4 * qc + 4)
                for tt in range(4):
                    t = 4 * qc + tt
                    if qc >= 1:
                        v.wait_ge(k.s_ntp.h, 8 * (4 * (qc - 1) + tt + 1))
                    k.s_n1.inc(v.tensor_scalar(
                        n1b[tt][:, :], o1w[:, tt * DM:(tt + 1) * DM],
                        cms[:, tt:tt + 1], cnw[:, tt:tt + 1],
                        ALU.subtract, ALU.mult))

    stC.close()


def emit_D(k):
    nc = k.nc
    stD = ExitStack()
    sb = lambda name, shape, dt: stD.enter_context(nc.sbuf_tensor(name, shape, dt))
    ps = lambda name, shape, dt: stD.enter_context(nc.psum_tensor(name, shape, dt))

    n1c = [sb(f"n1c{i}", [128, MC * 512], BF16) for i in range(2)]
    wrsb = sb("wrsb", [128, 64], BF16)
    wgb = [sb(f"wgb{i}", [128, 4096], BF16) for i in range(2)]
    wub = [sb(f"wub{i}", [128, 4096], BF16) for i in range(2)]
    wde = [sb(f"wde{i}", [128, 4096], BF16) for i in range(4)]
    sde = [sb(f"sde{i}", [128, 1024], BF16) for i in range(4)]
    sgc = [sb(f"sgc{i}", [128, 1024], BF16) for i in range(2)]
    suc = [sb(f"suc{i}", [128, 1024], BF16) for i in range(2)]
    hb = sb("hb", [128, 32 * 512], BF16)
    midb = sb("midb", [128, 8 * 512], BF16)
    lgt = sb("lgt", [8, 512], F32)
    ltok = sb("ltok", [128, 32], F32)
    exb = sb("exb", [128, 32], F32)
    mskb = sb("mskb", [128, 32], F32)
    selb = sb("selb", [128, 32], F32)
    rwn = sb("rwn", [128, 32], F32)
    rwtok = sb("rwtok", [128, 32], F32)
    nm1 = sb("nm1", [128, 4], F32)
    m1s = sb("m1s", [128, 4], F32)
    m2s = sb("m2s", [128, 4], F32)
    zs = sb("zs", [128, 4], F32)
    rzz = sb("rzz", [128, 4], F32)
    rwt = sb("rwt", [8, 512], BF16)
    rwbe = [sb(f"rwbe{i}", [128, 512], BF16) for i in range(2)]
    sgt = [sb(f"sgt{i}", [128, 512], BF16) for i in range(2)]
    ut = [sb(f"ut{i}", [128, 512], BF16) for i in range(2)]
    ffnt = sb("ffnt", [128, 8 * 512], BF16)
    ftokb = sb("ftokb", [128, 4 * DM], F32)
    o1rb = [sb(f"o1rb{i}", [128, DM], F32) for i in range(4)]
    ybuf = [sb(f"ybuf{i}", [128, DM], F32) for i in range(4)]
    fsum = sb("fsum", [128, 4], F32)
    fsq = sb("fsq", [128, 4], F32)
    fms = sb("fms", [128, 4], F32)
    fvp = sb("fvp", [128, 4], F32)
    frcs = sb("frcs", [128, 4], F32)
    fr0s = sb("fr0s", [128, 4], F32)
    fnw = sb("fnw", [128, 4], F32)
    scrD = sb("scrD", [128, DM], BF16)

    psg = [ps(f"psg{i}", [128, 512], F32) for i in range(2)]
    psu = [ps(f"psu{i}", [128, 512], F32) for i in range(2)]
    psr = ps("psr", [128, 512], F32)
    psT = ps("psT", [128, 512], F32)
    psf = [ps(f"psf{i}", [128, 1024], BF16) for i in range(2)]

    _mk_sems(k, ["s_wr", "s_rmm", "s_rev", "s_st",
                 "s_ltp", "s_lte", "s_nm1", "s_rex", "s_rwd", "s_rtp", "s_rte",
                 "s_rbm", "s_rbe", "s_gmm", "s_gsl", "s_uml", "s_hml",
                 "s_dmm", "s_dev", "s_ftp", "s_fte", "s_fsq", "s_frc",
                 "s_yv"])
    Sem = k.Sem
    k.s_n1l2 = [Sem("s_n1l0"), Sem("s_n1l1")]
    k.s_o1l4 = [Sem(f"s_o1l{i}") for i in range(4)]
    k.s_wg2 = [Sem("s_wg0"), Sem("s_wg1")]
    k.s_wd4 = [Sem(f"s_wd{i}") for i in range(4)]
    k.s_sg2 = [Sem("s_sg0"), Sem("s_sg1")]
    k.s_yo4 = [Sem(f"s_yo{i}") for i in range(4)]

    def psg_rel(u):
        T, j = u // 48, u % 48
        if j < 40:
            return (k.s_gsl, 40 * T + j + 1)
        return (k.s_dev, 8 * T + (j - 40) + 1)

    with nc.Block("phD") as blk:
        @blk.sync
        def _(sp):
            sp.wait_ge(k.s_n1d.h, k.s_n1d.n)
            sp.wait_ge(k.s_o1d.h, k.s_o1d.n)
            def ystores(T):
                for tt in range(4):
                    t = 4 * T + tt
                    sp.wait_ge(k.s_yv.h, t + 1)
                    k.s_yo4[tt].inc(sp.dma_start(
                        out=k.y[t * 128:(t + 1) * 128, :],
                        in_=ybuf[tt][:, :]), 16)

            for T in range(NCH):
                if T >= 2:
                    sp.wait_ge(k.s_gmm.h, 80 * (T - 1))
                k.s_n1l2[T % 2].inc(sp.dma_start(
                    out=n1c[T % 2][:, :],
                    in_=k.n1s[:, T * 4096:(T + 1) * 4096]), 16)
                for tt in range(4):
                    t = 4 * T + tt
                    if T >= 1:
                        sp.wait_ge(k.s_yv.h, 4 * (T - 1) + tt + 1)
                    k.s_o1l4[tt].inc(sp.dma_start(
                        out=o1rb[tt][:, :],
                        in_=k.o1s[t * 128:(t + 1) * 128, :]), 16)
                if T >= 1:
                    ystores(T - 1)
            ystores(NCH - 1)
            for tt in range(4):
                sp.wait_ge(k.s_yo4[tt].h, k.s_yo4[tt].n)

        @blk.gpsimd
        def _(g):
            k.s_wr.inc(g.dma_start(
                out=wrsb[:, :].rearrange("p (c e) -> p c e", c=8),
                in_=k.wr_d.rearrange("(c p) e -> p c e", p=128),
                allow_slow_non_contiguous=True), 16)

            def wgwu(T):
                for e in range(E):
                    ge = 8 * T + e
                    if ge >= 2:
                        g.wait_ge(k.s_gmm.h, 2 * (40 * ((ge - 2) // 8)
                                                   + 4 * ((ge - 2) % 8) + 4))
                    if T == 0:
                        g.wait_ge(k.s_cvs.h, 16 * (2 * e + 2))
                    k.s_wg2[ge % 2].inc(g.dma_start(
                        out=wgb[e % 2][:, :], in_=k.wbf_g[e, :, :]), 16)
                    k.s_wg2[ge % 2].inc(g.dma_start(
                        out=wub[e % 2][:, :], in_=k.wbf_u[e, :, :]), 16)

            def sgsu(T):
                for fsc in range(8):
                    gl = 8 * T + fsc
                    if gl >= 2:
                        pT, pf = (gl - 2) // 8, (gl - 2) % 8
                        g.wait_ge(k.s_gmm.h, 2 * (40 * pT + 32 + pf + 1))
                    if T == 0:
                        g.wait_ge(k.s_cvs.h, 16 * (18 + 2 * fsc))
                    k.s_sg2[gl % 2].inc(g.dma_start(
                        out=sgc[fsc % 2][:, :], in_=k.wbf_sg[fsc, :, :]), 16)
                    k.s_sg2[gl % 2].inc(g.dma_start(
                        out=suc[fsc % 2][:, :], in_=k.wbf_su[fsc, :, :]), 16)

            def wdsd(T):
                for dc in range(8):
                    gq = 8 * T + dc
                    if gq >= 4:
                        g.wait_ge(k.s_dmm.h, gq - 3)
                    if T == 0:
                        g.wait_ge(k.s_cvs.h, 16 * (34 + 2 * dc))
                    k.s_wd4[dc % 4].inc(g.dma_start(
                        out=wde[dc % 4][:, :], in_=k.wbf_d[dc, :, :]), 16)
                    k.s_wd4[dc % 4].inc(g.dma_start(
                        out=sde[dc % 4][:, :], in_=k.wbf_sd[dc, :, :]), 16)

            wgwu(0)
            sgsu(0)
            for T in range(NCH):
                wdsd(T)
                if T + 1 < NCH:
                    wgwu(T + 1)
                    sgsu(T + 1)
        @blk.tensor
        def _(te):
            te.wait_ge(k.s_wr.h, 16)

            def router_mm(J):
                te.wait_ge(k.s_n1l2[J % 2].h, 16 * (J // 2 + 1))
                te.wait_ge(k.s_rbe.h, 8 * J)
                for kc in range(MC):
                    mm = te.matmul(psr[0:8, :], wrsb[:, kc * 8:(kc + 1) * 8],
                                   n1c[J % 2][:, kc * 512:(kc + 1) * 512],
                                   start=(kc == 0), stop=(kc == MC - 1))
                k.s_rmm.inc(mm)
                te.wait_ge(k.s_rev.h, J + 1)
                if J >= 1:
                    te.wait_ge(k.s_rte.h, J)
                for tt in range(4):
                    k.s_ltp.inc(te.transpose(psT[:, tt * 8:(tt + 1) * 8],
                                             lgt[0:8, tt * 128:(tt + 1) * 128],
                                             k.idf[0:8, 0:8]))

            def router_rtp(J):
                te.wait_ge(k.s_rwd.h, J + 1)
                te.wait_ge(k.s_lte.h, J + 1)
                for tt in range(4):
                    k.s_rtp.inc(te.transpose(psT[0:8, tt * 128:(tt + 1) * 128],
                                             rwtok[:, tt * 8:(tt + 1) * 8],
                                             k.idf[:, :]))

            router_mm(0)
            router_rtp(0)
            for T in range(NCH):
                te.wait_ge(k.s_rte.h, T + 1)
                for e in range(E):
                    ge = 8 * T + e
                    if ge >= 1:
                        te.wait_ge(k.s_rbe.h, ge)
                    k.s_rbm.inc(te.matmul(psr[:, :],
                                          k.ohsb[0:8, e * 128:(e + 1) * 128],
                                          rwt[0:8, :], start=True, stop=True))
                    te.wait_ge(k.s_wg2[ge % 2].h, 32 * (ge // 2 + 1))
                    for fc in range(4):
                        u = 48 * T + 4 * e + fc
                        if u >= 2:
                            sem, val = psg_rel(u - 2)
                            te.wait_ge(sem.h, val)
                        for kc in range(MC):
                            mm = te.matmul(psg[u % 2][:, :],
                                           wgb[e % 2][:, kc * 512 + fc * 128:
                                                      kc * 512 + (fc + 1) * 128],
                                           n1c[T % 2][:, kc * 512:(kc + 1) * 512],
                                           start=(kc == 0), stop=(kc == MC - 1))
                        k.s_gmm.inc(mm)
                        v = 40 * T + 4 * e + fc
                        if v >= 2:
                            te.wait_ge(k.s_uml.h, v - 1)
                        for kc in range(MC):
                            mm = te.matmul(psu[v % 2][:, :],
                                           wub[e % 2][:, kc * 512 + fc * 128:
                                                      kc * 512 + (fc + 1) * 128],
                                           n1c[T % 2][:, kc * 512:(kc + 1) * 512],
                                           start=(kc == 0), stop=(kc == MC - 1))
                        k.s_gmm.inc(mm)
                for fsc in range(8):
                    gl0 = 8 * T + fsc
                    te.wait_ge(k.s_sg2[gl0 % 2].h, 32 * (gl0 // 2 + 1))
                    u = 48 * T + 32 + fsc
                    sem, val = psg_rel(u - 2)
                    te.wait_ge(sem.h, val)
                    for kc in range(MC):
                        mm = te.matmul(psg[u % 2][:, :],
                                       sgc[fsc % 2][:, kc * 128:(kc + 1) * 128],
                                       n1c[T % 2][:, kc * 512:(kc + 1) * 512],
                                       start=(kc == 0), stop=(kc == MC - 1))
                    k.s_gmm.inc(mm)
                    v = 40 * T + 32 + fsc
                    te.wait_ge(k.s_uml.h, v - 1)
                    for kc in range(MC):
                        mm = te.matmul(psu[v % 2][:, :],
                                       suc[fsc % 2][:, kc * 128:(kc + 1) * 128],
                                       n1c[T % 2][:, kc * 512:(kc + 1) * 512],
                                       start=(kc == 0), stop=(kc == MC - 1))
                    k.s_gmm.inc(mm)
                if T + 1 < NCH:
                    router_mm(T + 1)
                for dc in range(8):
                    gq = 8 * T + dc
                    te.wait_ge(k.s_wd4[dc % 4].h, 64 * T + 32 * (dc // 4 + 1))
                    u = 48 * T + 40 + dc
                    sem, val = psg_rel(u - 2)
                    te.wait_ge(sem.h, val)
                    first = True
                    for e in range(E):
                        for fc in range(4):
                            if dc == 0:
                                te.wait_ge(k.s_hml.h, 40 * T + 4 * e + fc + 1)
                            mm = te.matmul(
                                psg[u % 2][:, :],
                                wde[dc % 4][:, (e * 4 + fc) * 128:
                                            (e * 4 + fc + 1) * 128],
                                hb[:, (e * 4 + fc) * 512:
                                   (e * 4 + fc + 1) * 512],
                                start=first, stop=False)
                            first = False
                    for fsc in range(8):
                        if dc == 0:
                            te.wait_ge(k.s_hml.h, 40 * T + 32 + fsc + 1)
                        mm = te.matmul(
                            psg[u % 2][:, :],
                            sde[dc % 4][:, fsc * 128:(fsc + 1) * 128],
                            midb[:, fsc * 512:(fsc + 1) * 512],
                            start=False, stop=(fsc == 7))
                    k.s_dmm.inc(mm)
                    if dc == 1 and T + 1 < NCH:
                        router_rtp(T + 1)
                te.wait_ge(k.s_dev.h, 8 * (T + 1))
                for tt in range(4):
                    gidx = 4 * T + tt
                    if gidx >= 2:
                        te.wait_ge(k.s_fte.h, gidx - 1)
                    for dc in range(8):
                        k.s_ftp.inc(te.transpose(
                            psf[gidx % 2][:, dc * 128:(dc + 1) * 128],
                            ffnt[:, dc * 512 + tt * 128:
                                 dc * 512 + tt * 128 + 128],
                            k.idbf[:, :]))
        @blk.scalar
        def _(sc):
            def router_acts(J):
                sc.wait_ge(k.s_rmm.h, J + 1)
                k.s_rev.inc(sc.activation(out=lgt[0:8, :], in_=psr[0:8, :],
                                          func=AF.Identity, bias=k.ebsb[0:8, 0:1]))
                sc.wait_ge(k.s_ltp.h, 4 * (J + 1))
                k.s_lte.inc(sc.copy(out=ltok[:, :], in_=psT[:, 0:32]))
                sc.wait_ge(k.s_nm1.h, J + 1)
                for tt in range(4):
                    k.s_rex.inc(sc.activation(
                        out=exb[:, tt * 8:(tt + 1) * 8],
                        in_=ltok[:, tt * 8:(tt + 1) * 8], func=AF.Exp,
                        bias=nm1[:, tt:tt + 1]))

            def router_rte(J):
                sc.wait_ge(k.s_rtp.h, 4 * (J + 1))
                if J >= 1:
                    sc.wait_ge(k.s_rbm.h, 8 * J)
                with nc.allow_low_precision(reason="bf16 routing weights"):
                    k.s_rte.inc(sc.copy(out=rwt[0:8, :], in_=psT[0:8, :]))

            router_acts(0)
            router_rte(0)
            for T in range(NCH):
                for e in range(E):
                    ge = 8 * T + e
                    sc.wait_ge(k.s_rbm.h, ge + 1)
                    if ge >= 2:
                        sc.wait_ge(k.s_uml.h, 40 * ((ge - 2) // 8)
                                   + 4 * ((ge - 2) % 8) + 4)
                    with nc.allow_low_precision(reason="bf16 routing bcast"):
                        k.s_rbe.inc(sc.copy(out=rwbe[e % 2][:, :], in_=psr[:, :]))
                    for fc in range(4):
                        j = 4 * e + fc
                        v = 40 * T + j
                        sc.wait_ge(k.s_gmm.h, 2 * v + 1)
                        if v >= 2:
                            sc.wait_ge(k.s_hml.h, v - 1)
                        k.s_gsl.inc(sc.activation(out=sgt[v % 2][:, :],
                                                  in_=psg[(48 * T + j) % 2][:, :],
                                                  func=AF.Silu))
                for j in range(32, 40):
                    v = 40 * T + j
                    sc.wait_ge(k.s_gmm.h, 2 * v + 1)
                    sc.wait_ge(k.s_hml.h, v - 1)
                    k.s_gsl.inc(sc.activation(out=sgt[v % 2][:, :],
                                              in_=psg[(48 * T + j) % 2][:, :],
                                              func=AF.Silu))
                if T + 1 < NCH:
                    router_acts(T + 1)
                for dc in range(8):
                    sc.wait_ge(k.s_dmm.h, 8 * T + dc + 1)
                    with nc.allow_low_precision(reason="bf16 ffn pre-LN"):
                        k.s_dev.inc(sc.copy(out=ffnt[:, dc * 512:(dc + 1) * 512],
                                            in_=psg[(48 * T + 40 + dc) % 2][:, :]))
                    if dc == 1 and T + 1 < NCH:
                        router_rte(T + 1)
                for tt in range(4):
                    gidx = 4 * T + tt
                    sc.wait_ge(k.s_ftp.h, 8 * (gidx + 1))
                    k.s_fte.inc(sc.copy(
                        out=ftokb[:, tt * DM:(tt + 1) * DM],
                        in_=psf[gidx % 2][:, :]))
                for tt in range(4):
                    sc.drain()
                    k.s_fsq.inc(sc.activation(
                        out=scrD[:, :], in_=ftokb[:, tt * DM:(tt + 1) * DM],
                        func=AF.Square, accum_out=fsq[:, tt:tt + 1]))
                sc.wait_ge(k.s_frc.h, 2 * T + 1)
                sc.activation(out=frcs[:, :], in_=fvp[:, :], func=AF.Ln)
                sc.drain()
                k.s_frc.inc(sc.activation(out=fnw[:, :], in_=frcs[:, :],
                                          func=AF.Exp, scale=-0.5))
        @blk.vector
        def _(v):
            def top2(J):
                v.wait_ge(k.s_lte.h, J + 1)
                for tt in range(4):
                    v.reduce_max(nm1[:, tt:tt + 1], ltok[:, tt * 8:(tt + 1) * 8],
                                 axis=AX.X, negate=True)
                v.drain()
                k.s_nm1.inc(v.tensor_scalar(m1s[:, :], nm1[:, :], -1.0, None,
                                            ALU.mult, ALU.bypass))
                v.drain()
                for tt in range(4):
                    v.tensor_scalar(mskb[:, tt * 8:(tt + 1) * 8],
                                    ltok[:, tt * 8:(tt + 1) * 8],
                                    m1s[:, tt:tt + 1], -1e30, ALU.is_ge, ALU.mult)
                v.drain()
                v.tensor_tensor(mskb[:, :], ltok[:, :], mskb[:, :], ALU.add)
                v.drain()
                for tt in range(4):
                    v.reduce_max(m2s[:, tt:tt + 1], mskb[:, tt * 8:(tt + 1) * 8],
                                 axis=AX.X)
                v.drain()
                for tt in range(4):
                    v.tensor_scalar(selb[:, tt * 8:(tt + 1) * 8],
                                    ltok[:, tt * 8:(tt + 1) * 8],
                                    m2s[:, tt:tt + 1], None, ALU.is_ge, ALU.bypass)
                v.drain()
                v.wait_ge(k.s_rex.h, 4 * (J + 1))
                v.tensor_tensor(rwn[:, :], selb[:, :], exb[:, :], ALU.mult)
                v.drain()
                for tt in range(4):
                    v.reduce_sum(zs[:, tt:tt + 1], rwn[:, tt * 8:(tt + 1) * 8],
                                 axis=AX.X)
                v.drain()
                v.reciprocal(rzz[:, :], zs[:, :])
                v.drain()
                for tt in range(4):
                    v.tensor_scalar(rwtok[:, tt * 8:(tt + 1) * 8],
                                    rwn[:, tt * 8:(tt + 1) * 8],
                                    rzz[:, tt:tt + 1], None, ALU.mult, ALU.bypass)
                k.s_rwd.inc(v.drain())

            top2(0)
            for T in range(NCH):
                for e in range(E):
                    ge = 8 * T + e
                    for fc in range(4):
                        vv = 40 * T + 4 * e + fc
                        v.wait_ge(k.s_gmm.h, 2 * vv + 2)
                        v.wait_ge(k.s_rbe.h, ge + 1)
                        v.tensor_tensor(ut[vv % 2][:, :], psu[vv % 2][:, :],
                                        rwbe[e % 2][:, :], ALU.mult)
                        k.s_uml.inc(v.drain())
                        v.wait_ge(k.s_gsl.h, vv + 1)
                        v.tensor_tensor(hb[:, (4 * e + fc) * 512:
                                           (4 * e + fc + 1) * 512],
                                        sgt[vv % 2][:, :], ut[vv % 2][:, :],
                                        ALU.mult)
                        k.s_hml.inc(v.drain())
                for fsc in range(8):
                    vv = 40 * T + 32 + fsc
                    v.wait_ge(k.s_gmm.h, 2 * vv + 2)
                    v.wait_ge(k.s_gsl.h, vv + 1)
                    v.tensor_tensor(midb[:, fsc * 512:(fsc + 1) * 512],
                                    psu[vv % 2][:, :], sgt[vv % 2][:, :], ALU.mult)
                    k.s_uml.inc(v.drain())
                    k.s_hml.inc(v.nop())
                if T + 1 < NCH:
                    top2(T + 1)
                for tt in range(4):
                    v.wait_ge(k.s_fte.h, 4 * T + tt + 1)
                    v.reduce_sum(fsum[:, tt:tt + 1], ftokb[:, tt * DM:(tt + 1) * DM],
                                 axis=AX.X)
                v.wait_ge(k.s_fsq.h, 4 * (T + 1))
                v.drain()
                v.tensor_scalar(fms[:, :], fsum[:, :], 1.0 / DM, None,
                                ALU.mult, ALU.bypass)
                v.tensor_scalar(fvp[:, :], fsq[:, :], 1.0 / DM, EPS,
                                ALU.mult, ALU.add)
                v.drain()
                v.tensor_tensor(fnw[:, :], fms[:, :], fms[:, :], ALU.mult)
                v.drain()
                k.s_frc.inc(v.tensor_tensor(fvp[:, :], fvp[:, :], fnw[:, :],
                                            ALU.subtract))
                v.wait_ge(k.s_frc.h, 2 * T + 2)
                for tt in range(4):
                    t = 4 * T + tt
                    if T >= 1:
                        v.wait_ge(k.s_yo4[tt].h, 16 * T)
                    v.tensor_scalar(ybuf[tt][:, :], ftokb[:, tt * DM:(tt + 1) * DM],
                                    fms[:, tt:tt + 1], fnw[:, tt:tt + 1],
                                    ALU.subtract, ALU.mult)
                    v.drain()
                    v.wait_ge(k.s_o1l4[tt].h, 16 * (T + 1))
                    k.s_yv.inc(v.tensor_tensor(ybuf[tt][:, :], ybuf[tt][:, :],
                                               o1rb[tt][:, :], ALU.add))

    stD.close()


def kernel(**inputs):
    consts = host_consts()
    nc = build_nc()
    xfull = np.ascontiguousarray(np.asarray(inputs["L0_attn_x"]))
    shared = dict(consts)
    for n in WNAMES:
        shared[f"W{n}"] = np.ascontiguousarray(inputs[f"L0_attn_W{n}"])
        shared[f"b{n}"] = np.ascontiguousarray(inputs[f"L0_attn_b{n}"])
        shared[f"A{n}"] = np.ascontiguousarray(inputs[f"L0_attn_A{n}"])
        shared[f"B{n}"] = np.ascontiguousarray(inputs[f"L0_attn_B{n}"])
    shared["rz"] = np.ascontiguousarray(inputs["L0_attn_rezero"])
    shared["Wr"] = np.ascontiguousarray(inputs["L0_ffn_Wr"])
    shared["Wg"] = np.ascontiguousarray(inputs["L0_ffn_Wg"])
    shared["Wu"] = np.ascontiguousarray(inputs["L0_ffn_Wu"])
    shared["Wd"] = np.ascontiguousarray(inputs["L0_ffn_Wd"])
    shared["Sg"] = np.ascontiguousarray(inputs["L0_ffn_Sg"])
    shared["Su"] = np.ascontiguousarray(inputs["L0_ffn_Su"])
    shared["Sd"] = np.ascontiguousarray(inputs["L0_ffn_Sd"])
    shared["eb"] = np.ascontiguousarray(inputs["L0_ffn_expert_bias"])
    in_maps = [dict(shared, x=np.ascontiguousarray(xfull[b])) for b in range(8)]
    res = run_bass_kernel_spmd(nc, in_maps, core_ids=list(range(8)))
    return np.stack([res.results[b]["y"] for b in range(8)], axis=0)
